# version 10
# baseline (speedup 1.0000x reference)
import numpy as np
import concourse.bass as bass
import concourse.mybir as mybir
from concourse.bass_utils import run_bass_kernel_spmd

F32 = mybir.dt.float32
BF16 = mybir.dt.bfloat16
AF = mybir.ActivationFunctionType
OP = mybir.AluOpType

D = 1024
T = 2048
L = 256
NB = T // L
NCH = D // 128
NTI = L // 128
EPS = 1e-5


class Plan:

    ENGINES = ("tensor", "vector", "scalar", "gpsimd", "sync")

    def __init__(self):
        self.ops = []
        self.counts = {}
        self.regions = {}
        self.tok_group = {}

    def op(self, engine, emit, reads=(), writes=(), dma=False, tag="",
           group=None):
        if dma:
            self.counts["dq"] = self.counts.get("dq", 0) + 1
            sem = "dq%d" % self.counts["dq"]
            val = 16
        else:
            sem = engine
            val = self.counts.get(sem, 0) + 1
            self.counts[sem] = val
        token = (sem, val)
        deps = set()
        for r in reads:
            reg = self.regions.get(r)
            if reg and reg["w"]:
                deps.add(reg["w"])
            if reg and r.startswith("pp"):
                deps.update(t for t in reg["r"] if t[0] != sem)
        for w in writes:
            reg = self.regions.get(w)
            if reg:
                if reg["w"]:
                    deps.add(reg["w"])
                deps.update(reg["r"])
        if group is not None:
            deps = {d for d in deps if self.tok_group.get(d) != group}
            self.tok_group[token] = group
        self.ops.append(dict(engine=engine, emit=emit, sem=sem, val=val,
                             deps=sorted(deps), dma=dma, tag=tag))
        for r in reads:
            reg = self.regions.setdefault(r, {"w": None, "r": []})
            reg["r"].append(token)
        for w in writes:
            self.regions[w] = {"w": token, "r": []}
        return token

    def finalize(self):
        needed = {}
        for rec in self.ops:
            for (sem, val) in rec["deps"]:
                needed.setdefault(sem, set()).add(val)
        self.needed = {sem: sorted(v) for sem, v in needed.items()}
        self.incval = {}
        for sem, vals in self.needed.items():
            if sem.startswith("dq"):
                continue
            for i, v in enumerate(vals):
                self.incval[(sem, v)] = i + 1
        known = {}
        for rec in self.ops:
            w = []
            for (sem, val) in sorted(rec["deps"]):
                if sem.startswith("dq"):
                    cnt, wv = 16, 16
                else:
                    cnt = self.incval[(sem, val)]
                    wv = cnt
                k = known.setdefault(rec["engine"], {})
                if k.get(sem, 0) < cnt:
                    k[sem] = cnt
                    w.append((sem, wv))
            rec["waits"] = w
            if rec["sem"].startswith("dq"):
                rec["inc"] = True
            else:
                rec["inc"] = rec["val"] in set(self.needed.get(rec["sem"], []))


def build_program():
    nc = bass.Bass("TRN2", target_bir_lowering=False,
                   dynamic_dma_scratch_size=4096)

    x_d = nc.dram_tensor("x", [T, D], F32, kind="ExternalInput")
    gam_d = nc.dram_tensor("gamma", [128, NCH], F32, kind="ExternalInput")
    w_d = {}
    for n in ("wq", "wk", "wv", "wa", "wg", "wo", "wrT", "wkT", "wvT"):
        w_d[n] = nc.dram_tensor(n, [128, NCH, D], F32, kind="ExternalInput")
    mur_d = nc.dram_tensor("mu_r", [128, NCH], F32, kind="ExternalInput")
    muk_d = nc.dram_tensor("mu_k", [128, NCH], F32, kind="ExternalInput")
    id_d = nc.dram_tensor("ident", [128, 128], F32, kind="ExternalInput")
    out_d = nc.dram_tensor("out", [T, D], F32, kind="ExternalOutput")

    ctx = []

    def sb(name, shape, dtype):
        g = nc.sbuf_tensor(name, shape, dtype)
        t = g.__enter__()
        ctx.append(g)
        return t

    def psum(name, shape, dtype):
        g = nc.psum_tensor(name, shape, dtype)
        t = g.__enter__()
        ctx.append(g)
        return t

    wsb = {}
    for n in ("wq", "wk", "wv", "wa", "wg", "wo", "wrT", "wkT", "wvT"):
        wsb[n] = sb("W_" + n, [128, NCH, D], BF16)
    x_tm = sb("x_tm", [128, 3, NTI * D], F32)
    xhat_fm = sb("xhat_fm", [128, 2, NCH * L], BF16)
    xh2_fm = sb("xh2_fm", [128, NCH * L], BF16)
    bfP = sb("bfP", [128, NCH * L], BF16)
    y2b = sb("y2b", [128, NCH * L], BF16)
    fA = sb("fA", [128, NCH * L], F32)
    fB = sb("fB", [128, NCH * L], F32)
    fC = sb("fC", [128, NCH * L], F32)
    stats = sb("stats", [128, 36], F32)
    gam = sb("gam", [128, NCH], F32)
    mur = sb("mur", [128, NCH], F32)
    muk = sb("muk", [128, NCH], F32)
    onemr = sb("onemr", [128, NCH], F32)
    onemk = sb("onemk", [128, NCH], F32)
    carry = sb("carry", [128, NCH], F32)
    carry2 = sb("carry2", [128, NCH], BF16)
    ident = sb("identity", [128, 128], F32)

    pp = [psum("pp%d" % i, [128, 512], F32) for i in range(8)]

    p = Plan()

    def wview(n, kc, dlo, dhi):
        return wsb[n][:, kc, dlo:dhi]

    def xhat_view(b, c):
        return xhat_fm[:, b % 2, c * L:(c + 1) * L]

    def f3(t, c):
        return t[:, c * L:(c + 1) * L]

    def fhalf(t, j):
        return t[:, j * 512:(j + 1) * 512]

    def chunked(t):
        return t[:, :].rearrange("p (c l) -> p c l", c=NCH)

    BN = 0
    MV1, MV2 = 12, 16
    SS1, SS2 = 20, 22
    CU, CT, CR, CQ1 = 24, 25, 26, 27

    ppi = [0]

    def next_pps():
        base = ppi[0]
        ids = [(base + i) % 8 for i in range(4)]
        ppi[0] = (base + 4) % 8
        return ids

    def mm_stage(wname, rhs_fn, rhs_regions, tag):
        ids = next_pps()
        for j in range(4):
            pj = pp[ids[j]]
            for h in range(2):
                douc = 2 * j + h
                for kc in range(NCH):
                    p.op("tensor",
                         (lambda pj=pj, h=h, wname=wname, kc=kc, douc=douc,
                                 st=(kc == 0), sp=(kc == NCH - 1), rhs_fn=rhs_fn:
                          nc.tensor.matmul(pj[:, h * 256:(h + 1) * 256],
                                           wview(wname, kc, douc * 128, (douc + 1) * 128),
                                           rhs_fn(kc), start=st, stop=sp)),
                         reads=["W_" + wname] + list(rhs_regions),
                         writes=["pp%d" % ids[j]], tag=tag,
                         group=tag + "_p%d" % ids[j])
        return ids

    def transpose_stage(srcbuf, srcreg, tag, sub=False):
        ids = next_pps()
        for c in range(NCH):
            for ti in range(NTI):
                pj = pp[ids[2 * ti + c // 4]]
                src = chunked(srcbuf)[:, c, ti * 128:(ti + 1) * 128]
                rd = [("%s%d" % (srcreg, c // 2)) if sub else srcreg, "ident"]
                p.op("tensor", (lambda pj=pj, src=src, c=c: nc.tensor.transpose(
                    pj[:, (c % 4) * 128:(c % 4 + 1) * 128], src, ident[:, :])),
                     reads=rd,
                     writes=["pp%d" % ids[2 * ti + c // 4]], tag=tag,
                     group=tag + "_p%d" % ids[2 * ti + c // 4])
        return ids

    def resid_stage(b, t_ids, tag):
        par = b % 3
        for ti in range(NTI):
            for hh in range(2):
                j = 2 * ti + hh
                xv = x_tm[:, par, ti * D + hh * 512: ti * D + (hh + 1) * 512]
                p.op("vector", (lambda xv=xv, j=j, t_ids=t_ids:
                                nc.vector.tensor_tensor(xv, xv, pp[t_ids[j]][:, :], OP.add)),
                     reads=["pp%d" % t_ids[j], "x%d" % par], writes=["x%d" % par],
                     tag=tag)

    MV1, MV2, SS1, SS2 = 12, 16, 20, 22
    BB1, BB2 = 32, 34
    CUc, CTc, CRc, CQ1c = 24, 26, 28, 30

    def ln_stats(b, which, tag):
        par = b % 3
        base_mv = MV1 if which == 1 else MV2
        ssc = SS1 if which == 1 else SS2
        xreg = "x%d" % par
        reg = "stats%d" % which
        for ti in range(NTI):
            xt = x_tm[:, par, ti * D:(ti + 1) * D]
            p.op("vector", (lambda xt=xt: nc.vector.bn_stats(
                stats[:, 0:6], xt[:, 0:512])),
                 reads=[xreg], writes=["statsB"], tag=tag)
            p.op("vector", (lambda xt=xt: nc.vector.bn_stats(
                stats[:, 6:12], xt[:, 512:1024])),
                 reads=[xreg], writes=["statsB"], tag=tag)
            p.op("vector", (lambda base_mv=base_mv, ti=ti: nc.vector.bn_aggr(
                stats[:, base_mv + 2 * ti:base_mv + 2 * ti + 2], stats[:, 0:12])),
                 reads=["statsB"], writes=[reg, "statsB"], tag=tag)
        vv = stats[:, base_mv + 1:base_mv + 4:2]
        uu = stats[:, CUc:CUc + 2]
        if which == 1:
            p.op("vector", (lambda vv=vv, uu=uu: nc.vector.tensor_scalar(
                uu, vv, (D + EPS) / D, (EPS * EPS) / D, OP.mult, OP.add)),
                 reads=[reg], writes=["statsN"], tag=tag)
        else:
            p.op("vector", (lambda vv=vv, uu=uu: nc.vector.tensor_scalar(
                uu, vv, 1.0, EPS, OP.mult, OP.add)),
                 reads=[reg], writes=["statsN"], tag=tag)
        p.op("scalar", (lambda uu=uu: nc.scalar.activation(
            stats[:, CTc:CTc + 2], uu, AF.Sqrt)),
             reads=["statsN"], writes=["statsN"], tag=tag + "_sqrt")
        p.op("vector", (lambda: nc.vector.reciprocal(
            stats[:, CRc:CRc + 2], stats[:, CTc:CTc + 2])),
             reads=["statsN"], writes=["statsN"], tag=tag)
        p.op("vector", (lambda: nc.vector.tensor_tensor(
            stats[:, CQ1c:CQ1c + 2], stats[:, CRc:CRc + 2], stats[:, CRc:CRc + 2],
            OP.mult)),
             reads=["statsN"], writes=["statsN"], tag=tag)
        p.op("vector", (lambda uu=uu: nc.vector.tensor_tensor(
            stats[:, CTc:CTc + 2], uu, stats[:, CQ1c:CQ1c + 2], OP.mult)),
             reads=["statsN"], writes=["statsN"], tag=tag)
        p.op("vector", (lambda: nc.vector.tensor_scalar(
            stats[:, CTc:CTc + 2], stats[:, CTc:CTc + 2], -0.5, 1.5, OP.mult, OP.add)),
             reads=["statsN"], writes=["statsN"], tag=tag)
        p.op("vector", (lambda ssc=ssc: nc.vector.tensor_tensor(
            stats[:, ssc:ssc + 2], stats[:, CRc:CRc + 2], stats[:, CTc:CTc + 2],
            OP.mult)),
             reads=["statsN"], writes=[reg, "statsN"], tag=tag)
        bbc = BB1 if which == 1 else BB2
        mm_ap = stats[:, base_mv:base_mv + 4:2]
        p.op("vector", (lambda ssc=ssc, bbc=bbc, mm_ap=mm_ap:
                        nc.vector.scalar_tensor_tensor(
                            stats[:, bbc:bbc + 2], mm_ap, -1.0,
                            stats[:, ssc:ssc + 2], OP.mult, OP.mult)),
             reads=[reg], writes=[reg], tag=tag)

    def ln_apply(b, which, tag):
        par = b % 3
        ssc = SS1 if which == 1 else SS2
        xreg = "x%d" % par
        reg = "stats%d" % which
        dstreg = ("xhf%d" % (b % 2)) if which == 1 else "xh2"
        bbc = BB1 if which == 1 else BB2
        for ti in range(NTI):
            b_ap = stats[:, bbc + ti:bbc + ti + 1]
            s_ap = stats[:, ssc + ti:ssc + ti + 1]
            xt = x_tm[:, par, ti * D:(ti + 1) * D]
            p.op("scalar", (lambda xt=xt, b_ap=b_ap, s_ap=s_ap, ti=ti:
                            nc.scalar.activation(bfP[:, ti * D:(ti + 1) * D], xt,
                                                 AF.Identity, bias=b_ap, scale=s_ap)),
                 reads=[xreg, reg], writes=["bfP"], tag=tag)
        for ti in range(NTI):
            if which == 1:
                dst = xhat_fm[:, b % 2, :].rearrange(
                    "p (c l) -> p c l", c=NCH)[:, :, ti * 128:(ti + 1) * 128]
            else:
                dst = chunked(xh2_fm)[:, :, ti * 128:(ti + 1) * 128]
            p.op("scalar", (lambda dst=dst, ti=ti: nc.scalar.dma_start_transpose(
                dst, bfP[:, ti * D:(ti + 1) * D])),
                 reads=["bfP"], writes=[dstreg], dma=True, tag=tag + "_dmaT")

    def load_x(b, eng):
        par = b % 3
        for ti in range(NTI):
            srcap = x_d[b * L + ti * 128: b * L + (ti + 1) * 128, :]
            dst = x_tm[:, par, ti * D:(ti + 1) * D]
            p.op(eng, (lambda dst=dst, srcap=srcap, eng=eng:
                       getattr(nc, eng).dma_start(dst, srcap)),
                 reads=[], writes=["x%d" % par], dma=True,
                 tag="load_x%d" % b)

    def small_load(dst, src, reg):
        p.op("sync", (lambda dst=dst, src=src: nc.sync.dma_start(dst, src)),
             reads=[], writes=[reg], dma=True, tag="prep_small")

    load_x(0, "sync")
    load_x(1, "sync")
    load_x(2, "sync")
    small_load(gam[:, :], gam_d[:, :], "gam")
    small_load(mur[:, :], mur_d[:, :], "mur")
    small_load(muk[:, :], muk_d[:, :], "muk")
    small_load(ident[:, :], id_d[:, :], "ident")

    for n in ("wk", "wv", "wa", "wq", "wg", "wo", "wrT", "wkT", "wvT"):
        p.op("gpsimd", (lambda n=n:
                        nc.gpsimd.dma_start(wsb[n][:, :, :], w_d[n][:, :, :])),
             reads=[], writes=["W_" + n], dma=True, tag="cast_" + n)
        if n in ("wk", "wv", "wa", "wq", "wg"):
            for c in range(NCH):
                p.op("vector", (lambda n=n, c=c: nc.vector.tensor_scalar_mul(
                    wsb[n][:, c, :], wsb[n][:, c, :], gam[:, c:c + 1])),
                     reads=["gam", "W_" + n], writes=["W_" + n], tag="gfold_" + n)

    p.op("vector", (lambda: nc.vector.tensor_scalar(
        onemr[:, :], mur[:, :], -1.0, 1.0, OP.mult, OP.add)),
         reads=["mur"], writes=["onemr"], tag="onemr")
    p.op("vector", (lambda: nc.vector.tensor_scalar(
        onemk[:, :], muk[:, :], -1.0, 1.0, OP.mult, OP.add)),
         reads=["muk"], writes=["onemk"], tag="onemk")

    store_toks = []

    def phase_A(b):
        ln_stats(b, 1, "A%d" % b)
        ln_apply(b, 1, "A%d" % b)

    def phase_hoist_a(b):
        rr = ["xhf%d" % (b % 2)]
        rhs = lambda kc, b=b: xhat_view(b, kc)
        a_ids = mm_stage("wa", rhs, rr, "a%d" % b)
        for j in range(4):
            p.op("scalar", (lambda j=j, a_ids=a_ids: nc.scalar.activation(
                fhalf(fA, j), pp[a_ids[j]][:, :], AF.Sigmoid)),
                 reads=["pp%d" % a_ids[j]], writes=["fA%d" % j], tag="siga%d" % b)

    def phase_hoist_kvq_mm(b):
        rr = ["xhf%d" % (b % 2)]
        rhs = lambda kc, b=b: xhat_view(b, kc)
        k_ids = mm_stage("wk", rhs, rr, "k%d" % b)
        for j in range(4):
            p.op("scalar", (lambda j=j, k_ids=k_ids: nc.scalar.copy(
                fhalf(fB, j), pp[k_ids[j]][:, :])),
                 reads=["pp%d" % k_ids[j]], writes=["fB"], tag="kev%d" % b)
        v_ids = mm_stage("wv", rhs, rr, "v%d" % b)
        q_ids = mm_stage("wq", rhs, rr, "q%d" % b)
        return v_ids, q_ids

    def phase_kvev(b, v_ids):
        for j in range(4):
            p.op("vector", (lambda j=j, v_ids=v_ids: nc.vector.tensor_tensor(
                fhalf(fB, j), pp[v_ids[j]][:, :], fhalf(fB, j), OP.mult)),
                 reads=["pp%d" % v_ids[j], "fB"], writes=["fB"], tag="kv%d" % b)

    def phase_scan(b):
        for c in range(NCH):
            init = carry[:, c:c + 1] if b > 0 else 0.0
            rd = ["fA%d" % (c // 2), "fB"] + (["carry"] if b > 0 else [])
            p.op("vector", (lambda c=c, init=init: nc.vector.tensor_tensor_scan(
                f3(fC, c), f3(fA, c), f3(fB, c), init, OP.mult, OP.add)),
                 reads=rd, writes=["fC"], tag="scan%d" % b)
        p.op("vector", (lambda: nc.vector.tensor_copy(
            carry[:, :].rearrange("p (c o) -> p c o", o=1),
            chunked(fC)[:, :, L - 1:L])),
             reads=["fC"], writes=["carry"], tag="carryupd%d" % b)

    def phase_yq(b, q_ids):
        for j in range(4):
            p.op("vector", (lambda j=j, q_ids=q_ids: nc.vector.tensor_tensor(
                fhalf(fC, j), fhalf(fC, j), pp[q_ids[j]][:, :], OP.mult)),
                 reads=["pp%d" % q_ids[j], "fC"], writes=["fC"], tag="yq%d" % b)

    def phase_hoist_g(b):
        rr = ["xhf%d" % (b % 2)]
        rhs = lambda kc, b=b: xhat_view(b, kc)
        g_ids = mm_stage("wg", rhs, rr, "g%d" % b)
        for j in range(4):
            p.op("scalar", (lambda j=j, g_ids=g_ids: nc.scalar.activation(
                fhalf(bfP, j), pp[g_ids[j]][:, :], AF.Sigmoid)),
                 reads=["pp%d" % g_ids[j]], writes=["bfP"], tag="sigg%d" % b)
        return g_ids

    def phase_ygy2(b, g_ids):
        for j in range(4):
            p.op("vector", (lambda j=j, g_ids=g_ids: nc.vector.tensor_tensor(
                fhalf(fC, j), fhalf(fC, j), pp[g_ids[j]][:, :], OP.mult)),
                 reads=["pp%d" % g_ids[j], "fC"], writes=["fC"], tag="yg%d" % b)
        p.op("vector", (lambda: nc.vector.tensor_tensor(
            y2b[:, :], fC[:, :], bfP[:, :], OP.mult)),
             reads=["fC", "bfP"], writes=["y2b"], tag="y2_%d" % b)

    def phase_D(b):
        wo_ids = mm_stage("wo", (lambda kc: chunked(y2b)[:, kc, :]), ["y2b"],
                          "wo%d" % b)
        for j in range(4):
            p.op("scalar", (lambda j=j, wo_ids=wo_ids: nc.scalar.copy(
                fhalf(fA, j), pp[wo_ids[j]][:, :])),
                 reads=["pp%d" % wo_ids[j]], writes=["fA%d" % j], tag="woev%d" % b)
        t_ids = transpose_stage(fA, "fA", "Tattn%d" % b, sub=True)
        resid_stage(b, t_ids, "resid1_%d" % b)

    def phase_E1(b):
        ln_stats(b, 2, "E1_%d" % b)
        ln_apply(b, 2, "E1_%d" % b)

    def phase_E2(b):
        v = chunked(xh2_fm)
        pv = chunked(bfP)
        xr = chunked(y2b)
        p.op("vector", (lambda v=v, pv=pv: nc.vector.tensor_copy(
            pv[:, :, 1:L], v[:, :, 0:L - 1])),
             reads=["xh2"], writes=["bfP"], tag="prev%d" % b)
        if b == 0:
            p.op("vector", (lambda pv=pv: nc.vector.memset(pv[:, :, 0:1], 0.0)),
                 reads=[], writes=["bfP"], tag="prev0_%d" % b)
        else:
            p.op("vector", (lambda pv=pv: nc.vector.tensor_copy(
                pv[:, :, 0:1], carry2[:, :].rearrange("p (c o) -> p c o", o=1))),
                 reads=["carry2"], writes=["bfP"], tag="prev0_%d" % b)
        p.op("vector", (lambda v=v: nc.vector.tensor_copy(
            carry2[:, :].rearrange("p (c o) -> p c o", o=1), v[:, :, L - 1:L])),
             reads=["xh2"], writes=["carry2"], tag="carry2_%d" % b)
        for c in range(NCH):
            p.op("vector", (lambda c=c, v=v, xr=xr: nc.vector.tensor_scalar_mul(
                xr[:, c, :], v[:, c, :], mur[:, c:c + 1])),
                 reads=["xh2", "mur"], writes=["y2b"], tag="xr%d" % b)
            p.op("vector", (lambda c=c, pv=pv, xr=xr: nc.vector.scalar_tensor_tensor(
                xr[:, c, :], pv[:, c, :], onemr[:, c:c + 1], xr[:, c, :],
                OP.mult, OP.add)),
                 reads=["bfP", "onemr", "y2b"], writes=["y2b"], tag="xr%d" % b)
        for c in range(NCH):
            p.op("vector", (lambda c=c, v=v: nc.vector.tensor_scalar_mul(
                v[:, c, :], v[:, c, :], muk[:, c:c + 1])),
                 reads=["xh2", "muk"], writes=["xh2"], tag="xk%d" % b)
            p.op("vector", (lambda c=c, pv=pv, v=v: nc.vector.scalar_tensor_tensor(
                v[:, c, :], pv[:, c, :], onemk[:, c:c + 1], v[:, c, :],
                OP.mult, OP.add)),
                 reads=["bfP", "onemk", "xh2"], writes=["xh2"], tag="xk%d" % b)

    def phase_F1(b):
        xr_fn = lambda kc: chunked(y2b)[:, kc, :]
        r_ids = mm_stage("wrT", xr_fn, ["y2b"], "fr%d" % b)
        for j in range(4):
            p.op("scalar", (lambda j=j, r_ids=r_ids: nc.scalar.activation(
                fhalf(fB, j), pp[r_ids[j]][:, :], AF.Sigmoid)),
                 reads=["pp%d" % r_ids[j]], writes=["fB"], tag="sigr%d" % b)

    def phase_F2(b):
        xk_fn = lambda kc: chunked(xh2_fm)[:, kc, :]
        k_ids = mm_stage("wkT", xk_fn, ["xh2"], "fk%d" % b)
        for j in range(4):
            p.op("scalar", (lambda j=j, k_ids=k_ids: nc.scalar.activation(
                fhalf(y2b, j), pp[k_ids[j]][:, :], AF.Relu)),
                 reads=["pp%d" % k_ids[j]], writes=["y2b"], tag="krelu%d" % b)
        p.op("scalar", (lambda: nc.scalar.activation(
            xh2_fm[:, :], y2b[:, :], AF.Square)),
             reads=["y2b"], writes=["xh2"], tag="ksq%d" % b)

    def phase_F3(b):
        ks_fn = lambda kc: chunked(xh2_fm)[:, kc, :]
        v_ids = mm_stage("wvT", ks_fn, ["xh2"], "fv%d" % b)
        for j in range(4):
            p.op("vector", (lambda j=j, v_ids=v_ids: nc.vector.tensor_tensor(
                fhalf(fB, j), pp[v_ids[j]][:, :], fhalf(fB, j), OP.mult)),
                 reads=["pp%d" % v_ids[j], "fB"], writes=["fB"], tag="fvev%d" % b)

    def phase_F4(b):
        t_ids = transpose_stage(fB, "fB", "Tffn%d" % b)
        resid_stage(b, t_ids, "resid2_%d" % b)

    def phase_G(b):
        par = b % 3
        dst = out_d[b * L:(b + 1) * L, :].rearrange("(t p) d -> p t d", p=128)
        srcv = x_tm[:, par, :].rearrange("p (t d) -> p t d", t=NTI)
        store_toks.append(
            p.op("sync", (lambda dst=dst, srcv=srcv: nc.sync.dma_start(dst, srcv)),
                 reads=["x%d" % par], writes=[], dma=True, tag="store%d" % b))
        if b + 3 < NB:
            load_x(b + 3, "sync")

    phase_A(0)
    vq = {}
    g_ids_of = {}
    vq[0] = phase_hoist_kvq_mm(0)
    phase_kvev(0, vq[0][0])
    phase_hoist_a(0)
    phase_scan(0)
    phase_yq(0, vq[0][1])
    g_ids_of[0] = phase_hoist_g(0)
    phase_ygy2(0, g_ids_of[0])
    phase_A(1)
    for b in range(NB):
        phase_D(b)
        if b + 1 < NB:
            phase_hoist_a(b + 1)
        phase_E1(b)
        if b + 1 < NB:
            vq[b + 1] = phase_hoist_kvq_mm(b + 1)
        phase_E2(b)
        if b + 1 < NB:
            phase_kvev(b + 1, vq[b + 1][0])
            phase_scan(b + 1)
            phase_yq(b + 1, vq[b + 1][1])
        phase_F1(b)
        phase_F2(b)
        if b + 1 < NB:
            g_ids_of[b + 1] = phase_hoist_g(b + 1)
            phase_ygy2(b + 1, g_ids_of[b + 1])
        if b + 2 < NB:
            phase_A(b + 2)
        phase_F3(b)
        phase_F4(b)
        phase_G(b)
        if b + 2 < NB:
            phase_A(b + 2)

    p.finalize()

    sems = {}
    sem_guards = []
    sem_names = ["tensor", "vector", "scalar", "gpsimd", "sync"]
    sem_names += ["dq%d" % (i + 1) for i in range(p.counts.get("dq", 0))]
    for name in sem_names:
        g = nc.semaphore("sem_" + name)
        sems[name] = g.__enter__()
        sem_guards.append(g)

    per_engine = {e: [] for e in Plan.ENGINES}
    for rec in p.ops:
        per_engine[rec["engine"]].append(rec)

    final_waits = [(sem, 16) for (sem, _v) in store_toks]

    with nc.Block() as block:
        def run_engine(eng_name, eng):
            for rec in per_engine[eng_name]:
                for (sem, val) in rec["waits"]:
                    eng.wait_ge(sems[sem], val)
                ins = rec["emit"]()
                if rec["inc"]:
                    step = 16 if rec["sem"].startswith("dq") else 1
                    ins.then_inc(sems[rec["sem"]], step)

        @block.tensor
        def _(eng):
            run_engine("tensor", eng)

        @block.vector
        def _(eng):
            run_engine("vector", eng)

        @block.scalar
        def _(eng):
            run_engine("scalar", eng)

        @block.gpsimd
        def _(eng):
            run_engine("gpsimd", eng)

        @block.sync
        def _(eng):
            run_engine("sync", eng)
            for (sem, val) in final_waits:
                eng.wait_ge(sems[sem], val)

    for g in reversed(sem_guards):
        g.__exit__(None, None, None)
    for g in reversed(ctx):
        g.__exit__(None, None, None)
    return nc


def kernel(L0_attn_x, L0_attn_gamma, L0_attn_wq, L0_attn_wk, L0_attn_wv,
           L0_attn_wa, L0_attn_wg, L0_attn_wo, L0_ffn_W_r, L0_ffn_W_k,
           L0_ffn_W_v, L0_ffn_mu_r, L0_ffn_mu_k):
    x = np.ascontiguousarray(np.asarray(L0_attn_x, dtype=np.float32))
    def lay(w):
        return np.ascontiguousarray(
            np.asarray(w, np.float32).reshape(8, 128, 1024).transpose(1, 0, 2))

    com = {
        "gamma": np.ascontiguousarray(
            np.asarray(L0_attn_gamma, np.float32).reshape(8, 128).T),
        "wq": lay(L0_attn_wq),
        "wk": lay(L0_attn_wk),
        "wv": lay(L0_attn_wv),
        "wa": lay(L0_attn_wa),
        "wg": lay(L0_attn_wg),
        "wo": lay(L0_attn_wo),
        "wrT": lay(np.asarray(L0_ffn_W_r, np.float32).T),
        "wkT": lay(np.asarray(L0_ffn_W_k, np.float32).T),
        "wvT": lay(np.asarray(L0_ffn_W_v, np.float32).T),
        "mu_r": np.ascontiguousarray(
            np.asarray(L0_ffn_mu_r, np.float32).reshape(8, 128).T),
        "mu_k": np.ascontiguousarray(
            np.asarray(L0_ffn_mu_k, np.float32).reshape(8, 128).T),
        "ident": np.eye(128, dtype=np.float32),
    }
    nc = build_program()
    in_maps = [dict(com, x=np.ascontiguousarray(x[c])) for c in range(8)]
    res = run_bass_kernel_spmd(nc, in_maps, core_ids=list(range(8)))
    return np.stack([np.asarray(r["out"], dtype=np.float32)
                     for r in res.results], axis=0)


# version 11
# speedup vs baseline: 1.0727x; 1.0727x over previous
import numpy as np
import concourse.bass as bass
import concourse.mybir as mybir
from concourse.bass_utils import run_bass_kernel_spmd

F32 = mybir.dt.float32
BF16 = mybir.dt.bfloat16
AF = mybir.ActivationFunctionType
OP = mybir.AluOpType

D = 1024
T = 2048
L = 256
NB = T // L
NCH = D // 128
NTI = L // 128
EPS = 1e-5


class Plan:

    ENGINES = ("tensor", "vector", "scalar", "gpsimd", "sync")

    def __init__(self):
        self.ops = []
        self.counts = {}
        self.regions = {}
        self.tok_group = {}

    def op(self, engine, emit, reads=(), writes=(), dma=False, tag="",
           group=None):
        if dma:
            self.counts["dq"] = self.counts.get("dq", 0) + 1
            sem = "dq%d" % self.counts["dq"]
            val = 16
        else:
            sem = engine
            val = self.counts.get(sem, 0) + 1
            self.counts[sem] = val
        token = (sem, val)
        deps = set()
        for r in reads:
            reg = self.regions.get(r)
            if reg and reg["w"]:
                deps.add(reg["w"])
            if reg and r.startswith("pp"):
                deps.update(t for t in reg["r"] if t[0] != sem)
        for w in writes:
            reg = self.regions.get(w)
            if reg:
                if reg["w"]:
                    deps.add(reg["w"])
                deps.update(reg["r"])
        if group is not None:
            deps = {d for d in deps if self.tok_group.get(d) != group}
            self.tok_group[token] = group
        self.ops.append(dict(engine=engine, emit=emit, sem=sem, val=val,
                             deps=sorted(deps), dma=dma, tag=tag))
        for r in reads:
            reg = self.regions.setdefault(r, {"w": None, "r": []})
            reg["r"].append(token)
        for w in writes:
            self.regions[w] = {"w": token, "r": []}
        return token

    def finalize(self):
        needed = {}
        for rec in self.ops:
            for (sem, val) in rec["deps"]:
                needed.setdefault(sem, set()).add(val)
        self.needed = {sem: sorted(v) for sem, v in needed.items()}
        self.incval = {}
        for sem, vals in self.needed.items():
            if sem.startswith("dq"):
                continue
            for i, v in enumerate(vals):
                self.incval[(sem, v)] = i + 1
        known = {}
        for rec in self.ops:
            w = []
            for (sem, val) in sorted(rec["deps"]):
                if sem.startswith("dq"):
                    cnt, wv = 16, 16
                else:
                    cnt = self.incval[(sem, val)]
                    wv = cnt
                k = known.setdefault(rec["engine"], {})
                if k.get(sem, 0) < cnt:
                    k[sem] = cnt
                    w.append((sem, wv))
            rec["waits"] = w
            if rec["sem"].startswith("dq"):
                rec["inc"] = True
            else:
                rec["inc"] = rec["val"] in set(self.needed.get(rec["sem"], []))


def build_program():
    nc = bass.Bass("TRN2", target_bir_lowering=False,
                   dynamic_dma_scratch_size=4096)

    x_d = nc.dram_tensor("x", [T, D], F32, kind="ExternalInput")
    gam_d = nc.dram_tensor("gamma", [128, NCH], F32, kind="ExternalInput")
    w_d = {}
    for n in ("wq", "wk", "wv", "wa", "wg", "wo", "wrT", "wkT", "wvT"):
        w_d[n] = nc.dram_tensor(n, [128, NCH, D], F32, kind="ExternalInput")
    mur_d = nc.dram_tensor("mu_r", [128, NCH], F32, kind="ExternalInput")
    muk_d = nc.dram_tensor("mu_k", [128, NCH], F32, kind="ExternalInput")
    id_d = nc.dram_tensor("ident", [128, 128], F32, kind="ExternalInput")
    out_d = nc.dram_tensor("out", [T, D], F32, kind="ExternalOutput")

    ctx = []

    def sb(name, shape, dtype):
        g = nc.sbuf_tensor(name, shape, dtype)
        t = g.__enter__()
        ctx.append(g)
        return t

    def psum(name, shape, dtype):
        g = nc.psum_tensor(name, shape, dtype)
        t = g.__enter__()
        ctx.append(g)
        return t

    wsb = {}
    for n in ("wq", "wk", "wv", "wa", "wg", "wo", "wrT", "wkT", "wvT"):
        wsb[n] = sb("W_" + n, [128, NCH, D], BF16)
    x_tm = sb("x_tm", [128, 3, NTI * D], F32)
    xhat_fm = sb("xhat_fm", [128, 2, NCH * L], BF16)
    xh2_fm = sb("xh2_fm", [128, NCH * L], BF16)
    bfP = sb("bfP", [128, NCH * L], BF16)
    stg1 = sb("stg1", [128, NTI * D], BF16)
    y2b = sb("y2b", [128, NCH * L], BF16)
    fA = sb("fA", [128, NCH * L], F32)
    fB = sb("fB", [128, NCH * L], F32)
    fC = sb("fC", [128, NCH * L], F32)
    stats = sb("stats", [128, 36], F32)
    gam = sb("gam", [128, NCH], F32)
    mur = sb("mur", [128, NCH], F32)
    muk = sb("muk", [128, NCH], F32)
    onemr = sb("onemr", [128, NCH], F32)
    onemk = sb("onemk", [128, NCH], F32)
    carry = sb("carry", [128, NCH], F32)
    carry2 = sb("carry2", [128, NCH], BF16)
    ident = sb("identity", [128, 128], F32)

    pp = [psum("pp%d" % i, [128, 512], F32) for i in range(8)]

    p = Plan()

    def wview(n, kc, dlo, dhi):
        return wsb[n][:, kc, dlo:dhi]

    def xhat_view(b, c):
        return xhat_fm[:, b % 2, c * L:(c + 1) * L]

    def f3(t, c):
        return t[:, c * L:(c + 1) * L]

    def fhalf(t, j):
        return t[:, j * 512:(j + 1) * 512]

    def chunked(t):
        return t[:, :].rearrange("p (c l) -> p c l", c=NCH)

    BN = 0
    MV1, MV2 = 12, 16
    SS1, SS2 = 20, 22
    CU, CT, CR, CQ1 = 24, 25, 26, 27

    ppi = [0]

    def next_pps():
        base = ppi[0]
        ids = [(base + i) % 8 for i in range(4)]
        ppi[0] = (base + 4) % 8
        return ids

    def mm_stage(wname, rhs_fn, rhs_regions, tag):
        ids = next_pps()
        for j in range(4):
            pj = pp[ids[j]]
            for h in range(2):
                douc = 2 * j + h
                for kc in range(NCH):
                    p.op("tensor",
                         (lambda pj=pj, h=h, wname=wname, kc=kc, douc=douc,
                                 st=(kc == 0), sp=(kc == NCH - 1), rhs_fn=rhs_fn:
                          nc.tensor.matmul(pj[:, h * 256:(h + 1) * 256],
                                           wview(wname, kc, douc * 128, (douc + 1) * 128),
                                           rhs_fn(kc), start=st, stop=sp)),
                         reads=["W_" + wname] + list(rhs_regions),
                         writes=["pp%d" % ids[j]], tag=tag,
                         group=tag + "_p%d" % ids[j])
        return ids

    def transpose_stage(srcbuf, srcreg, tag, sub=False):
        ids = next_pps()
        for c in range(NCH):
            for ti in range(NTI):
                pj = pp[ids[2 * ti + c // 4]]
                src = chunked(srcbuf)[:, c, ti * 128:(ti + 1) * 128]
                rd = [("%s%d" % (srcreg, c // 2)) if sub else srcreg, "ident"]
                p.op("tensor", (lambda pj=pj, src=src, c=c: nc.tensor.transpose(
                    pj[:, (c % 4) * 128:(c % 4 + 1) * 128], src, ident[:, :])),
                     reads=rd,
                     writes=["pp%d" % ids[2 * ti + c // 4]], tag=tag,
                     group=tag + "_p%d" % ids[2 * ti + c // 4])
        return ids

    def resid_stage(b, t_ids, tag):
        par = b % 3
        for ti in range(NTI):
            for hh in range(2):
                j = 2 * ti + hh
                xv = x_tm[:, par, ti * D + hh * 512: ti * D + (hh + 1) * 512]
                p.op("vector", (lambda xv=xv, j=j, t_ids=t_ids:
                                nc.vector.tensor_tensor(xv, xv, pp[t_ids[j]][:, :], OP.add)),
                     reads=["pp%d" % t_ids[j], "x%d" % par], writes=["x%d" % par],
                     tag=tag)

    MV1, MV2, SS1, SS2 = 12, 16, 20, 22
    BB1, BB2 = 32, 34
    CUc, CTc, CRc, CQ1c = 24, 26, 28, 30

    def ln_stats(b, which, tag):
        par = b % 3
        base_mv = MV1 if which == 1 else MV2
        ssc = SS1 if which == 1 else SS2
        xreg = "x%d" % par
        reg = "stats%d" % which
        for ti in range(NTI):
            xt = x_tm[:, par, ti * D:(ti + 1) * D]
            p.op("vector", (lambda xt=xt: nc.vector.bn_stats(
                stats[:, 0:6], xt[:, 0:512])),
                 reads=[xreg], writes=["statsB"], tag=tag)
            p.op("vector", (lambda xt=xt: nc.vector.bn_stats(
                stats[:, 6:12], xt[:, 512:1024])),
                 reads=[xreg], writes=["statsB"], tag=tag)
            p.op("vector", (lambda base_mv=base_mv, ti=ti: nc.vector.bn_aggr(
                stats[:, base_mv + 2 * ti:base_mv + 2 * ti + 2], stats[:, 0:12])),
                 reads=["statsB"], writes=[reg, "statsB"], tag=tag)
        vv = stats[:, base_mv + 1:base_mv + 4:2]
        uu = stats[:, CUc:CUc + 2]
        if which == 1:
            p.op("vector", (lambda vv=vv, uu=uu: nc.vector.tensor_scalar(
                uu, vv, (D + EPS) / D, (EPS * EPS) / D, OP.mult, OP.add)),
                 reads=[reg], writes=["statsN"], tag=tag)
        else:
            p.op("vector", (lambda vv=vv, uu=uu: nc.vector.tensor_scalar(
                uu, vv, 1.0, EPS, OP.mult, OP.add)),
                 reads=[reg], writes=["statsN"], tag=tag)
        p.op("scalar", (lambda uu=uu: nc.scalar.activation(
            stats[:, CTc:CTc + 2], uu, AF.Sqrt)),
             reads=["statsN"], writes=["statsN"], tag=tag + "_sqrt")
        p.op("vector", (lambda: nc.vector.reciprocal(
            stats[:, CRc:CRc + 2], stats[:, CTc:CTc + 2])),
             reads=["statsN"], writes=["statsN"], tag=tag)
        p.op("vector", (lambda: nc.vector.tensor_tensor(
            stats[:, CQ1c:CQ1c + 2], stats[:, CRc:CRc + 2], stats[:, CRc:CRc + 2],
            OP.mult)),
             reads=["statsN"], writes=["statsN"], tag=tag)
        p.op("vector", (lambda uu=uu: nc.vector.tensor_tensor(
            stats[:, CTc:CTc + 2], uu, stats[:, CQ1c:CQ1c + 2], OP.mult)),
             reads=["statsN"], writes=["statsN"], tag=tag)
        p.op("vector", (lambda: nc.vector.tensor_scalar(
            stats[:, CTc:CTc + 2], stats[:, CTc:CTc + 2], -0.5, 1.5, OP.mult, OP.add)),
             reads=["statsN"], writes=["statsN"], tag=tag)
        p.op("vector", (lambda ssc=ssc: nc.vector.tensor_tensor(
            stats[:, ssc:ssc + 2], stats[:, CRc:CRc + 2], stats[:, CTc:CTc + 2],
            OP.mult)),
             reads=["statsN"], writes=[reg, "statsN"], tag=tag)
        bbc = BB1 if which == 1 else BB2
        mm_ap = stats[:, base_mv:base_mv + 4:2]
        p.op("vector", (lambda ssc=ssc, bbc=bbc, mm_ap=mm_ap:
                        nc.vector.scalar_tensor_tensor(
                            stats[:, bbc:bbc + 2], mm_ap, -1.0,
                            stats[:, ssc:ssc + 2], OP.mult, OP.mult)),
             reads=[reg], writes=[reg], tag=tag)

    def ln_apply(b, which, tag):
        par = b % 3
        ssc = SS1 if which == 1 else SS2
        xreg = "x%d" % par
        reg = "stats%d" % which
        dstreg = ("xhf%d" % (b % 2)) if which == 1 else "xh2"
        bbc = BB1 if which == 1 else BB2
        stg = stg1 if which == 1 else bfP
        streg = "stg1" if which == 1 else "bfP"
        for ti in range(NTI):
            b_ap = stats[:, bbc + ti:bbc + ti + 1]
            s_ap = stats[:, ssc + ti:ssc + ti + 1]
            xt = x_tm[:, par, ti * D:(ti + 1) * D]
            p.op("scalar", (lambda xt=xt, b_ap=b_ap, s_ap=s_ap, ti=ti, stg=stg:
                            nc.scalar.activation(stg[:, ti * D:(ti + 1) * D], xt,
                                                 AF.Identity, bias=b_ap, scale=s_ap)),
                 reads=[xreg, reg], writes=[streg], tag=tag)
        for ti in range(NTI):
            if which == 1:
                dst = xhat_fm[:, b % 2, :].rearrange(
                    "p (c l) -> p c l", c=NCH)[:, :, ti * 128:(ti + 1) * 128]
            else:
                dst = chunked(xh2_fm)[:, :, ti * 128:(ti + 1) * 128]
            p.op("scalar", (lambda dst=dst, ti=ti, stg=stg:
                            nc.scalar.dma_start_transpose(
                                dst, stg[:, ti * D:(ti + 1) * D])),
                 reads=[streg], writes=[dstreg], dma=True, tag=tag + "_dmaT")

    def load_x(b, eng):
        par = b % 3
        for ti in range(NTI):
            srcap = x_d[b * L + ti * 128: b * L + (ti + 1) * 128, :]
            dst = x_tm[:, par, ti * D:(ti + 1) * D]
            p.op(eng, (lambda dst=dst, srcap=srcap, eng=eng:
                       getattr(nc, eng).dma_start(dst, srcap)),
                 reads=[], writes=["x%d" % par], dma=True,
                 tag="load_x%d" % b)

    def small_load(dst, src, reg):
        p.op("sync", (lambda dst=dst, src=src: nc.sync.dma_start(dst, src)),
             reads=[], writes=[reg], dma=True, tag="prep_small")

    load_x(0, "sync")
    load_x(1, "sync")
    load_x(2, "sync")
    small_load(gam[:, :], gam_d[:, :], "gam")
    small_load(mur[:, :], mur_d[:, :], "mur")
    small_load(muk[:, :], muk_d[:, :], "muk")
    small_load(ident[:, :], id_d[:, :], "ident")

    for n in ("wk", "wv", "wa", "wq", "wg", "wo", "wrT", "wkT", "wvT"):
        p.op("gpsimd", (lambda n=n:
                        nc.gpsimd.dma_start(wsb[n][:, :, :], w_d[n][:, :, :])),
             reads=[], writes=["W_" + n], dma=True, tag="cast_" + n)
        if n in ("wk", "wv", "wa", "wq", "wg"):
            for c in range(NCH):
                p.op("vector", (lambda n=n, c=c: nc.vector.tensor_scalar_mul(
                    wsb[n][:, c, :], wsb[n][:, c, :], gam[:, c:c + 1])),
                     reads=["gam", "W_" + n], writes=["W_" + n], tag="gfold_" + n)

    p.op("vector", (lambda: nc.vector.tensor_scalar(
        onemr[:, :], mur[:, :], -1.0, 1.0, OP.mult, OP.add)),
         reads=["mur"], writes=["onemr"], tag="onemr")
    p.op("vector", (lambda: nc.vector.tensor_scalar(
        onemk[:, :], muk[:, :], -1.0, 1.0, OP.mult, OP.add)),
         reads=["muk"], writes=["onemk"], tag="onemk")

    store_toks = []

    def phase_A(b):
        ln_stats(b, 1, "A%d" % b)
        ln_apply(b, 1, "A%d" % b)

    def phase_hoist_a(b):
        rr = ["xhf%d" % (b % 2)]
        rhs = lambda kc, b=b: xhat_view(b, kc)
        a_ids = mm_stage("wa", rhs, rr, "a%d" % b)
        for j in range(4):
            p.op("scalar", (lambda j=j, a_ids=a_ids: nc.scalar.activation(
                fhalf(fA, j), pp[a_ids[j]][:, :], AF.Sigmoid)),
                 reads=["pp%d" % a_ids[j]], writes=["fA%d" % j], tag="siga%d" % b)

    def phase_hoist_kvq_mm(b):
        rr = ["xhf%d" % (b % 2)]
        rhs = lambda kc, b=b: xhat_view(b, kc)
        k_ids = mm_stage("wk", rhs, rr, "k%d" % b)
        for j in range(4):
            p.op("scalar", (lambda j=j, k_ids=k_ids: nc.scalar.copy(
                fhalf(fB, j), pp[k_ids[j]][:, :])),
                 reads=["pp%d" % k_ids[j]], writes=["fB%d" % j], tag="kev%d" % b)
        v_ids = mm_stage("wv", rhs, rr, "v%d" % b)
        q_ids = mm_stage("wq", rhs, rr, "q%d" % b)
        return v_ids, q_ids

    def phase_kvev(b, v_ids):
        for j in range(4):
            p.op("vector", (lambda j=j, v_ids=v_ids: nc.vector.tensor_tensor(
                fhalf(fB, j), pp[v_ids[j]][:, :], fhalf(fB, j), OP.mult)),
                 reads=["pp%d" % v_ids[j], "fB%d" % j], writes=["fB%d" % j],
                 tag="kv%d" % b)

    def phase_scan(b):
        for c in range(NCH):
            init = carry[:, c:c + 1] if b > 0 else 0.0
            rd = ["fA%d" % (c // 2), "fB%d" % (c // 2)] + (["carry"] if b > 0 else [])
            p.op("vector", (lambda c=c, init=init: nc.vector.tensor_tensor_scan(
                f3(fC, c), f3(fA, c), f3(fB, c), init, OP.mult, OP.add)),
                 reads=rd, writes=["fC"], tag="scan%d" % b)
        p.op("vector", (lambda: nc.vector.tensor_copy(
            carry[:, :].rearrange("p (c o) -> p c o", o=1),
            chunked(fC)[:, :, L - 1:L])),
             reads=["fC"], writes=["carry"], tag="carryupd%d" % b)

    def phase_yq(b, q_ids):
        for j in range(4):
            p.op("vector", (lambda j=j, q_ids=q_ids: nc.vector.tensor_tensor(
                fhalf(fC, j), fhalf(fC, j), pp[q_ids[j]][:, :], OP.mult)),
                 reads=["pp%d" % q_ids[j], "fC"], writes=["fC"], tag="yq%d" % b)

    def phase_hoist_g(b):
        rr = ["xhf%d" % (b % 2)]
        rhs = lambda kc, b=b: xhat_view(b, kc)
        g_ids = mm_stage("wg", rhs, rr, "g%d" % b)
        for j in range(4):
            p.op("scalar", (lambda j=j, g_ids=g_ids: nc.scalar.activation(
                fhalf(bfP, j), pp[g_ids[j]][:, :], AF.Sigmoid)),
                 reads=["pp%d" % g_ids[j]], writes=["bfP"], tag="sigg%d" % b)
        return g_ids

    def phase_ygy2(b, g_ids):
        for j in range(4):
            p.op("vector", (lambda j=j, g_ids=g_ids: nc.vector.tensor_tensor(
                fhalf(fC, j), fhalf(fC, j), pp[g_ids[j]][:, :], OP.mult)),
                 reads=["pp%d" % g_ids[j], "fC"], writes=["fC"], tag="yg%d" % b)
        p.op("vector", (lambda: nc.vector.tensor_tensor(
            y2b[:, :], fC[:, :], bfP[:, :], OP.mult)),
             reads=["fC", "bfP"], writes=["y2b"], tag="y2_%d" % b)

    def phase_D(b):
        wo_ids = mm_stage("wo", (lambda kc: chunked(y2b)[:, kc, :]), ["y2b"],
                          "wo%d" % b)
        for j in range(4):
            p.op("scalar", (lambda j=j, wo_ids=wo_ids: nc.scalar.copy(
                fhalf(fB, j), pp[wo_ids[j]][:, :])),
                 reads=["pp%d" % wo_ids[j]], writes=["fB%d" % j], tag="woev%d" % b)
        t_ids = transpose_stage(fB, "fB", "Tattn%d" % b, sub=True)
        resid_stage(b, t_ids, "resid1_%d" % b)

    def phase_E1(b):
        ln_stats(b, 2, "E1_%d" % b)
        ln_apply(b, 2, "E1_%d" % b)

    def phase_E2(b):
        v = chunked(xh2_fm)
        pv = chunked(bfP)
        xr = chunked(y2b)
        p.op("vector", (lambda v=v, pv=pv: nc.vector.tensor_copy(
            pv[:, :, 1:L], v[:, :, 0:L - 1])),
             reads=["xh2"], writes=["bfP"], tag="prev%d" % b)
        if b == 0:
            p.op("vector", (lambda pv=pv: nc.vector.memset(pv[:, :, 0:1], 0.0)),
                 reads=[], writes=["bfP"], tag="prev0_%d" % b)
        else:
            p.op("vector", (lambda pv=pv: nc.vector.tensor_copy(
                pv[:, :, 0:1], carry2[:, :].rearrange("p (c o) -> p c o", o=1))),
                 reads=["carry2"], writes=["bfP"], tag="prev0_%d" % b)
        p.op("vector", (lambda v=v: nc.vector.tensor_copy(
            carry2[:, :].rearrange("p (c o) -> p c o", o=1), v[:, :, L - 1:L])),
             reads=["xh2"], writes=["carry2"], tag="carry2_%d" % b)
        for c in range(NCH):
            p.op("vector", (lambda c=c, v=v, xr=xr: nc.vector.tensor_scalar_mul(
                xr[:, c, :], v[:, c, :], mur[:, c:c + 1])),
                 reads=["xh2", "mur"], writes=["y2b"], tag="xr%d" % b)
            p.op("vector", (lambda c=c, pv=pv, xr=xr: nc.vector.scalar_tensor_tensor(
                xr[:, c, :], pv[:, c, :], onemr[:, c:c + 1], xr[:, c, :],
                OP.mult, OP.add)),
                 reads=["bfP", "onemr", "y2b"], writes=["y2b"], tag="xr%d" % b)
        for c in range(NCH):
            p.op("vector", (lambda c=c, v=v: nc.vector.tensor_scalar_mul(
                v[:, c, :], v[:, c, :], muk[:, c:c + 1])),
                 reads=["xh2", "muk"], writes=["xh2"], tag="xk%d" % b)
            p.op("vector", (lambda c=c, pv=pv, v=v: nc.vector.scalar_tensor_tensor(
                v[:, c, :], pv[:, c, :], onemk[:, c:c + 1], v[:, c, :],
                OP.mult, OP.add)),
                 reads=["bfP", "onemk", "xh2"], writes=["xh2"], tag="xk%d" % b)

    def phase_F1(b):
        xr_fn = lambda kc: chunked(y2b)[:, kc, :]
        r_ids = mm_stage("wrT", xr_fn, ["y2b"], "fr%d" % b)
        for j in range(4):
            p.op("scalar", (lambda j=j, r_ids=r_ids: nc.scalar.activation(
                fhalf(fB, j), pp[r_ids[j]][:, :], AF.Sigmoid)),
                 reads=["pp%d" % r_ids[j]], writes=["fB%d" % j], tag="sigr%d" % b)

    def phase_F2(b):
        xk_fn = lambda kc: chunked(xh2_fm)[:, kc, :]
        k_ids = mm_stage("wkT", xk_fn, ["xh2"], "fk%d" % b)
        for j in range(4):
            p.op("scalar", (lambda j=j, k_ids=k_ids: nc.scalar.activation(
                fhalf(y2b, j), pp[k_ids[j]][:, :], AF.Relu)),
                 reads=["pp%d" % k_ids[j]], writes=["y2b"], tag="krelu%d" % b)
        p.op("scalar", (lambda: nc.scalar.activation(
            xh2_fm[:, :], y2b[:, :], AF.Square)),
             reads=["y2b"], writes=["xh2"], tag="ksq%d" % b)

    def phase_F3(b):
        ks_fn = lambda kc: chunked(xh2_fm)[:, kc, :]
        v_ids = mm_stage("wvT", ks_fn, ["xh2"], "fv%d" % b)
        for j in range(4):
            p.op("vector", (lambda j=j, v_ids=v_ids: nc.vector.tensor_tensor(
                fhalf(fB, j), pp[v_ids[j]][:, :], fhalf(fB, j), OP.mult)),
                 reads=["pp%d" % v_ids[j], "fB%d" % j], writes=["fB%d" % j],
                 tag="fvev%d" % b)

    def phase_F4(b):
        t_ids = transpose_stage(fB, "fB", "Tffn%d" % b, sub=True)
        resid_stage(b, t_ids, "resid2_%d" % b)

    def phase_G(b):
        par = b % 3
        dst = out_d[b * L:(b + 1) * L, :].rearrange("(t p) d -> p t d", p=128)
        srcv = x_tm[:, par, :].rearrange("p (t d) -> p t d", t=NTI)
        store_toks.append(
            p.op("sync", (lambda dst=dst, srcv=srcv: nc.sync.dma_start(dst, srcv)),
                 reads=["x%d" % par], writes=[], dma=True, tag="store%d" % b))
        if b + 3 < NB:
            load_x(b + 3, "sync")

    phase_A(0)
    phase_A(1)
    vq = {}
    g_ids_of = {}
    vq[0] = phase_hoist_kvq_mm(0)
    phase_kvev(0, vq[0][0])
    phase_hoist_a(0)
    phase_scan(0)
    phase_yq(0, vq[0][1])
    g_ids_of[0] = phase_hoist_g(0)
    phase_ygy2(0, g_ids_of[0])
    phase_hoist_a(1)
    for b in range(NB):
        phase_D(b)
        phase_E1(b)
        if b + 1 < NB:
            vq[b + 1] = phase_hoist_kvq_mm(b + 1)
        phase_E2(b)
        if b + 2 < NB:
            phase_A(b + 2)
        if b + 1 < NB:
            phase_kvev(b + 1, vq[b + 1][0])
            phase_scan(b + 1)
            phase_yq(b + 1, vq[b + 1][1])
        phase_F1(b)
        phase_F2(b)
        if b + 1 < NB:
            g_ids_of[b + 1] = phase_hoist_g(b + 1)
            phase_ygy2(b + 1, g_ids_of[b + 1])
        phase_F3(b)
        phase_F4(b)
        if b + 2 < NB:
            phase_hoist_a(b + 2)
        phase_G(b)
        if b + 2 < NB:
            phase_A(b + 2)

    p.finalize()

    sems = {}
    sem_guards = []
    sem_names = ["tensor", "vector", "scalar", "gpsimd", "sync"]
    sem_names += ["dq%d" % (i + 1) for i in range(p.counts.get("dq", 0))]
    for name in sem_names:
        g = nc.semaphore("sem_" + name)
        sems[name] = g.__enter__()
        sem_guards.append(g)

    per_engine = {e: [] for e in Plan.ENGINES}
    for rec in p.ops:
        per_engine[rec["engine"]].append(rec)

    final_waits = [(sem, 16) for (sem, _v) in store_toks]

    with nc.Block() as block:
        def run_engine(eng_name, eng):
            for rec in per_engine[eng_name]:
                for (sem, val) in rec["waits"]:
                    eng.wait_ge(sems[sem], val)
                ins = rec["emit"]()
                if rec["inc"]:
                    step = 16 if rec["sem"].startswith("dq") else 1
                    ins.then_inc(sems[rec["sem"]], step)

        @block.tensor
        def _(eng):
            run_engine("tensor", eng)

        @block.vector
        def _(eng):
            run_engine("vector", eng)

        @block.scalar
        def _(eng):
            run_engine("scalar", eng)

        @block.gpsimd
        def _(eng):
            run_engine("gpsimd", eng)

        @block.sync
        def _(eng):
            run_engine("sync", eng)
            for (sem, val) in final_waits:
                eng.wait_ge(sems[sem], val)

    for g in reversed(sem_guards):
        g.__exit__(None, None, None)
    for g in reversed(ctx):
        g.__exit__(None, None, None)
    return nc


def kernel(L0_attn_x, L0_attn_gamma, L0_attn_wq, L0_attn_wk, L0_attn_wv,
           L0_attn_wa, L0_attn_wg, L0_attn_wo, L0_ffn_W_r, L0_ffn_W_k,
           L0_ffn_W_v, L0_ffn_mu_r, L0_ffn_mu_k):
    x = np.ascontiguousarray(np.asarray(L0_attn_x, dtype=np.float32))
    def lay(w):
        return np.ascontiguousarray(
            np.asarray(w, np.float32).reshape(8, 128, 1024).transpose(1, 0, 2))

    com = {
        "gamma": np.ascontiguousarray(
            np.asarray(L0_attn_gamma, np.float32).reshape(8, 128).T),
        "wq": lay(L0_attn_wq),
        "wk": lay(L0_attn_wk),
        "wv": lay(L0_attn_wv),
        "wa": lay(L0_attn_wa),
        "wg": lay(L0_attn_wg),
        "wo": lay(L0_attn_wo),
        "wrT": lay(np.asarray(L0_ffn_W_r, np.float32).T),
        "wkT": lay(np.asarray(L0_ffn_W_k, np.float32).T),
        "wvT": lay(np.asarray(L0_ffn_W_v, np.float32).T),
        "mu_r": np.ascontiguousarray(
            np.asarray(L0_ffn_mu_r, np.float32).reshape(8, 128).T),
        "mu_k": np.ascontiguousarray(
            np.asarray(L0_ffn_mu_k, np.float32).reshape(8, 128).T),
        "ident": np.eye(128, dtype=np.float32),
    }
    nc = build_program()
    in_maps = [dict(com, x=np.ascontiguousarray(x[c])) for c in range(8)]
    res = run_bass_kernel_spmd(nc, in_maps, core_ids=list(range(8)))
    return np.stack([np.asarray(r["out"], dtype=np.float32)
                     for r in res.results], axis=0)


# version 12
# speedup vs baseline: 1.1092x; 1.0341x over previous
import numpy as np
import concourse.bass as bass
import concourse.mybir as mybir
from concourse.bass_utils import run_bass_kernel_spmd

F32 = mybir.dt.float32
BF16 = mybir.dt.bfloat16
AF = mybir.ActivationFunctionType
OP = mybir.AluOpType

D = 1024
T = 2048
L = 256
NB = T // L
NCH = D // 128
NTI = L // 128
EPS = 1e-5


class Plan:

    ENGINES = ("tensor", "vector", "scalar", "gpsimd", "sync")

    def __init__(self):
        self.ops = []
        self.counts = {}
        self.regions = {}
        self.tok_group = {}

    def op(self, engine, emit, reads=(), writes=(), dma=False, tag="",
           group=None):
        if dma:
            self.counts["dq"] = self.counts.get("dq", 0) + 1
            sem = "dq%d" % self.counts["dq"]
            val = 16
        else:
            sem = engine
            val = self.counts.get(sem, 0) + 1
            self.counts[sem] = val
        token = (sem, val)
        deps = set()
        for r in reads:
            reg = self.regions.get(r)
            if reg and reg["w"]:
                deps.add(reg["w"])
            if reg and r.startswith("pp"):
                deps.update(t for t in reg["r"] if t[0] != sem)
        for w in writes:
            reg = self.regions.get(w)
            if reg:
                if reg["w"]:
                    deps.add(reg["w"])
                deps.update(reg["r"])
        if group is not None:
            deps = {d for d in deps if self.tok_group.get(d) != group}
            self.tok_group[token] = group
        self.ops.append(dict(engine=engine, emit=emit, sem=sem, val=val,
                             deps=sorted(deps), dma=dma, tag=tag))
        for r in reads:
            reg = self.regions.setdefault(r, {"w": None, "r": []})
            reg["r"].append(token)
        for w in writes:
            self.regions[w] = {"w": token, "r": []}
        return token

    def finalize(self):
        needed = {}
        for rec in self.ops:
            for (sem, val) in rec["deps"]:
                needed.setdefault(sem, set()).add(val)
        self.needed = {sem: sorted(v) for sem, v in needed.items()}
        self.incval = {}
        for sem, vals in self.needed.items():
            if sem.startswith("dq"):
                continue
            for i, v in enumerate(vals):
                self.incval[(sem, v)] = i + 1
        known = {}
        for rec in self.ops:
            w = []
            for (sem, val) in sorted(rec["deps"]):
                if sem.startswith("dq"):
                    cnt, wv = 16, 16
                else:
                    cnt = self.incval[(sem, val)]
                    wv = cnt
                k = known.setdefault(rec["engine"], {})
                if k.get(sem, 0) < cnt:
                    k[sem] = cnt
                    w.append((sem, wv))
            rec["waits"] = w
            if rec["sem"].startswith("dq"):
                rec["inc"] = True
            else:
                rec["inc"] = rec["val"] in set(self.needed.get(rec["sem"], []))


def build_program():
    nc = bass.Bass("TRN2", target_bir_lowering=False,
                   dynamic_dma_scratch_size=4096)

    x_d = nc.dram_tensor("x", [T, D], F32, kind="ExternalInput")
    gam_d = nc.dram_tensor("gamma", [128, NCH], F32, kind="ExternalInput")
    w_d = {}
    for n in ("wq", "wk", "wv", "wa", "wg", "wo", "wrT", "wkT", "wvT"):
        w_d[n] = nc.dram_tensor(n, [128, NCH, D], F32, kind="ExternalInput")
    mur_d = nc.dram_tensor("mu_r", [128, NCH], F32, kind="ExternalInput")
    muk_d = nc.dram_tensor("mu_k", [128, NCH], F32, kind="ExternalInput")
    id_d = nc.dram_tensor("ident", [128, 128], F32, kind="ExternalInput")
    out_d = nc.dram_tensor("out", [T, D], F32, kind="ExternalOutput")

    ctx = []

    def sb(name, shape, dtype):
        g = nc.sbuf_tensor(name, shape, dtype)
        t = g.__enter__()
        ctx.append(g)
        return t

    def psum(name, shape, dtype):
        g = nc.psum_tensor(name, shape, dtype)
        t = g.__enter__()
        ctx.append(g)
        return t

    wsb = {}
    for n in ("wq", "wk", "wv", "wa", "wg", "wo", "wrT", "wkT", "wvT"):
        wsb[n] = sb("W_" + n, [128, NCH, D], BF16)
    x_tm = sb("x_tm", [128, 3, NTI * D], F32)
    xhat_fm = sb("xhat_fm", [128, 2, NCH * L], BF16)
    xh2_fm = sb("xh2_fm", [128, NCH * L], BF16)
    bfP = sb("bfP", [128, NCH * L], BF16)
    stg1 = sb("stg1", [128, NTI * D], BF16)
    y2b = sb("y2b", [128, NCH * L], BF16)
    fA = sb("fA", [128, NCH * L], F32)
    fB = sb("fB", [128, NCH * L], F32)
    fC = sb("fC", [128, NCH * L], F32)
    stats = sb("stats", [128, 36], F32)
    gam = sb("gam", [128, NCH], F32)
    mur = sb("mur", [128, NCH], F32)
    muk = sb("muk", [128, NCH], F32)
    onemr = sb("onemr", [128, NCH], F32)
    onemk = sb("onemk", [128, NCH], F32)
    carry = sb("carry", [128, NCH], F32)
    carry2 = sb("carry2", [128, NCH], BF16)
    ident = sb("identity", [128, 128], F32)

    pp = [psum("pp%d" % i, [128, 512], F32) for i in range(8)]

    p = Plan()

    def wview(n, kc, dlo, dhi):
        return wsb[n][:, kc, dlo:dhi]

    def xhat_view(b, c):
        return xhat_fm[:, b % 2, c * L:(c + 1) * L]

    def f3(t, c):
        return t[:, c * L:(c + 1) * L]

    def fhalf(t, j):
        return t[:, j * 512:(j + 1) * 512]

    def chunked(t):
        return t[:, :].rearrange("p (c l) -> p c l", c=NCH)

    BN = 0
    MV1, MV2 = 12, 16
    SS1, SS2 = 20, 22
    CU, CT, CR, CQ1 = 24, 25, 26, 27

    ppi = [0]

    def next_pps():
        base = ppi[0]
        ids = [(base + i) % 8 for i in range(4)]
        ppi[0] = (base + 4) % 8
        return ids

    def mm_stage(wname, rhs_fn, rhs_regions, tag):
        ids = next_pps()
        for j in range(4):
            pj = pp[ids[j]]
            for h in range(2):
                douc = 2 * j + h
                for kc in range(NCH):
                    p.op("tensor",
                         (lambda pj=pj, h=h, wname=wname, kc=kc, douc=douc,
                                 st=(kc == 0), sp=(kc == NCH - 1), rhs_fn=rhs_fn:
                          nc.tensor.matmul(pj[:, h * 256:(h + 1) * 256],
                                           wview(wname, kc, douc * 128, (douc + 1) * 128),
                                           rhs_fn(kc), start=st, stop=sp)),
                         reads=["W_" + wname] + list(rhs_regions),
                         writes=["pp%d" % ids[j]], tag=tag,
                         group=tag + "_p%d" % ids[j])
        return ids

    def transpose_stage(srcbuf, srcreg, tag, sub=False):
        ids = next_pps()
        for c in range(NCH):
            for ti in range(NTI):
                pj = pp[ids[2 * ti + c // 4]]
                src = chunked(srcbuf)[:, c, ti * 128:(ti + 1) * 128]
                rd = [("%s%d" % (srcreg, c // 2)) if sub else srcreg, "ident"]
                p.op("tensor", (lambda pj=pj, src=src, c=c: nc.tensor.transpose(
                    pj[:, (c % 4) * 128:(c % 4 + 1) * 128], src, ident[:, :])),
                     reads=rd,
                     writes=["pp%d" % ids[2 * ti + c // 4]], tag=tag,
                     group=tag + "_p%d" % ids[2 * ti + c // 4])
        return ids

    def resid_stage(b, t_ids, tag):
        par = b % 3
        for ti in range(NTI):
            for hh in range(2):
                j = 2 * ti + hh
                xv = x_tm[:, par, ti * D + hh * 512: ti * D + (hh + 1) * 512]
                xrg = "x%dt%d" % (par, ti)
                p.op("vector", (lambda xv=xv, j=j, t_ids=t_ids:
                                nc.vector.tensor_tensor(xv, xv, pp[t_ids[j]][:, :], OP.add)),
                     reads=["pp%d" % t_ids[j], xrg], writes=[xrg],
                     tag=tag)

    MV1, MV2, SS1, SS2 = 12, 16, 20, 22
    BB1, BB2 = 32, 34
    CUc, CTc, CRc, CQ1c = 24, 26, 28, 30

    def ln_stats(b, which, tag):
        par = b % 3
        base_mv = MV1 if which == 1 else MV2
        ssc = SS1 if which == 1 else SS2
        xreg = "x%d" % par
        reg = "stats%d" % which
        for ti in range(NTI):
            xt = x_tm[:, par, ti * D:(ti + 1) * D]
            xrg = "%st%d" % (xreg, ti)
            p.op("vector", (lambda xt=xt: nc.vector.bn_stats(
                stats[:, 0:6], xt[:, 0:512])),
                 reads=[xrg], writes=["statsB"], tag=tag)
            p.op("vector", (lambda xt=xt: nc.vector.bn_stats(
                stats[:, 6:12], xt[:, 512:1024])),
                 reads=[xrg], writes=["statsB"], tag=tag)
            p.op("vector", (lambda base_mv=base_mv, ti=ti: nc.vector.bn_aggr(
                stats[:, base_mv + 2 * ti:base_mv + 2 * ti + 2], stats[:, 0:12])),
                 reads=["statsB"], writes=[reg, "statsB"], tag=tag)
        vv = stats[:, base_mv + 1:base_mv + 4:2]
        uu = stats[:, CUc:CUc + 2]
        if which == 1:
            p.op("vector", (lambda vv=vv, uu=uu: nc.vector.tensor_scalar(
                uu, vv, (D + EPS) / D, (EPS * EPS) / D, OP.mult, OP.add)),
                 reads=[reg], writes=["statsN"], tag=tag)
        else:
            p.op("vector", (lambda vv=vv, uu=uu: nc.vector.tensor_scalar(
                uu, vv, 1.0, EPS, OP.mult, OP.add)),
                 reads=[reg], writes=["statsN"], tag=tag)
        p.op("scalar", (lambda uu=uu: nc.scalar.activation(
            stats[:, CTc:CTc + 2], uu, AF.Sqrt)),
             reads=["statsN"], writes=["statsN"], tag=tag + "_sqrt")
        p.op("vector", (lambda: nc.vector.reciprocal(
            stats[:, CRc:CRc + 2], stats[:, CTc:CTc + 2])),
             reads=["statsN"], writes=["statsN"], tag=tag)
        p.op("vector", (lambda: nc.vector.tensor_tensor(
            stats[:, CQ1c:CQ1c + 2], stats[:, CRc:CRc + 2], stats[:, CRc:CRc + 2],
            OP.mult)),
             reads=["statsN"], writes=["statsN"], tag=tag)
        p.op("vector", (lambda uu=uu: nc.vector.tensor_tensor(
            stats[:, CTc:CTc + 2], uu, stats[:, CQ1c:CQ1c + 2], OP.mult)),
             reads=["statsN"], writes=["statsN"], tag=tag)
        p.op("vector", (lambda: nc.vector.tensor_scalar(
            stats[:, CTc:CTc + 2], stats[:, CTc:CTc + 2], -0.5, 1.5, OP.mult, OP.add)),
             reads=["statsN"], writes=["statsN"], tag=tag)
        p.op("vector", (lambda ssc=ssc: nc.vector.tensor_tensor(
            stats[:, ssc:ssc + 2], stats[:, CRc:CRc + 2], stats[:, CTc:CTc + 2],
            OP.mult)),
             reads=["statsN"], writes=[reg, "statsN"], tag=tag)
        bbc = BB1 if which == 1 else BB2
        mm_ap = stats[:, base_mv:base_mv + 4:2]
        p.op("vector", (lambda ssc=ssc, bbc=bbc, mm_ap=mm_ap:
                        nc.vector.scalar_tensor_tensor(
                            stats[:, bbc:bbc + 2], mm_ap, -1.0,
                            stats[:, ssc:ssc + 2], OP.mult, OP.mult)),
             reads=[reg], writes=[reg], tag=tag)

    def ln_apply(b, which, tag):
        par = b % 3
        ssc = SS1 if which == 1 else SS2
        xreg = "x%d" % par
        reg = "stats%d" % which
        dstreg = ("xhf%d" % (b % 2)) if which == 1 else "xh2"
        bbc = BB1 if which == 1 else BB2
        stg = stg1 if which == 1 else bfP
        streg = "stg1" if which == 1 else "bfP"
        for ti in range(NTI):
            b_ap = stats[:, bbc + ti:bbc + ti + 1]
            m_ap = stats[:, (MV1 if which == 1 else MV2) + 2 * ti:
                         (MV1 if which == 1 else MV2) + 2 * ti + 1]
            s_ap = stats[:, ssc + ti:ssc + ti + 1]
            xt = x_tm[:, par, ti * D:(ti + 1) * D]
            if ti == 0:
                p.op("vector", (lambda xt=xt, m_ap=m_ap, s_ap=s_ap, stg=stg:
                                nc.vector.tensor_scalar(stg[:, 0:D], xt, m_ap, s_ap,
                                                        OP.subtract, OP.mult)),
                     reads=[xreg + "t0", reg], writes=[streg], tag=tag)
            else:
                p.op("scalar", (lambda xt=xt, b_ap=b_ap, s_ap=s_ap, stg=stg:
                                nc.scalar.activation(stg[:, D:2 * D], xt,
                                                     AF.Identity, bias=b_ap, scale=s_ap)),
                     reads=[xreg + "t1", reg], writes=[streg], tag=tag)
        for ti in range(NTI):
            if which == 1:
                dst = xhat_fm[:, b % 2, :].rearrange(
                    "p (c l) -> p c l", c=NCH)[:, :, ti * 128:(ti + 1) * 128]
            else:
                dst = chunked(xh2_fm)[:, :, ti * 128:(ti + 1) * 128]
            p.op("sync", (lambda dst=dst, ti=ti, stg=stg:
                          nc.sync.dma_start_transpose(
                              dst, stg[:, ti * D:(ti + 1) * D])),
                 reads=[streg], writes=[dstreg], dma=True, tag=tag + "_dmaT")

    def load_x(b, eng):
        par = b % 3
        for ti in range(NTI):
            srcap = x_d[b * L + ti * 128: b * L + (ti + 1) * 128, :]
            dst = x_tm[:, par, ti * D:(ti + 1) * D]
            p.op(eng, (lambda dst=dst, srcap=srcap, eng=eng:
                       getattr(nc, eng).dma_start(dst, srcap)),
                 reads=[], writes=["x%dt%d" % (par, ti)], dma=True,
                 tag="load_x%d" % b)

    def small_load(dst, src, reg):
        p.op("sync", (lambda dst=dst, src=src: nc.sync.dma_start(dst, src)),
             reads=[], writes=[reg], dma=True, tag="prep_small")

    load_x(0, "sync")
    load_x(1, "sync")
    load_x(2, "sync")
    small_load(gam[:, :], gam_d[:, :], "gam")
    small_load(mur[:, :], mur_d[:, :], "mur")
    small_load(muk[:, :], muk_d[:, :], "muk")
    small_load(ident[:, :], id_d[:, :], "ident")

    for n in ("wk", "wv", "wa", "wq", "wg", "wo", "wrT", "wkT", "wvT"):
        p.op("gpsimd", (lambda n=n:
                        nc.gpsimd.dma_start(wsb[n][:, :, :], w_d[n][:, :, :])),
             reads=[], writes=["W_" + n], dma=True, tag="cast_" + n)
        if n in ("wk", "wv", "wa", "wq", "wg"):
            for c in range(NCH):
                p.op("vector", (lambda n=n, c=c: nc.vector.tensor_scalar_mul(
                    wsb[n][:, c, :], wsb[n][:, c, :], gam[:, c:c + 1])),
                     reads=["gam", "W_" + n], writes=["W_" + n], tag="gfold_" + n)

    p.op("vector", (lambda: nc.vector.tensor_scalar(
        onemr[:, :], mur[:, :], -1.0, 1.0, OP.mult, OP.add)),
         reads=["mur"], writes=["onemr"], tag="onemr")
    p.op("vector", (lambda: nc.vector.tensor_scalar(
        onemk[:, :], muk[:, :], -1.0, 1.0, OP.mult, OP.add)),
         reads=["muk"], writes=["onemk"], tag="onemk")

    store_toks = []

    def phase_A(b):
        ln_stats(b, 1, "A%d" % b)
        ln_apply(b, 1, "A%d" % b)

    def phase_hoist_a(b):
        rr = ["xhf%d" % (b % 2)]
        rhs = lambda kc, b=b: xhat_view(b, kc)
        a_ids = mm_stage("wa", rhs, rr, "a%d" % b)
        for j in range(4):
            p.op("scalar", (lambda j=j, a_ids=a_ids: nc.scalar.activation(
                fhalf(fA, j), pp[a_ids[j]][:, :], AF.Sigmoid)),
                 reads=["pp%d" % a_ids[j]], writes=["fA%d" % j], tag="siga%d" % b)

    def phase_hoist_kvq_mm(b):
        rr = ["xhf%d" % (b % 2)]
        rhs = lambda kc, b=b: xhat_view(b, kc)
        k_ids = mm_stage("wk", rhs, rr, "k%d" % b)
        for j in range(4):
            p.op("scalar", (lambda j=j, k_ids=k_ids: nc.scalar.copy(
                fhalf(fB, j), pp[k_ids[j]][:, :])),
                 reads=["pp%d" % k_ids[j]], writes=["fB%d" % j], tag="kev%d" % b)
        v_ids = mm_stage("wv", rhs, rr, "v%d" % b)
        q_ids = mm_stage("wq", rhs, rr, "q%d" % b)
        return v_ids, q_ids

    def phase_kvev(b, v_ids):
        for j in range(4):
            p.op("vector", (lambda j=j, v_ids=v_ids: nc.vector.tensor_tensor(
                fhalf(fB, j), pp[v_ids[j]][:, :], fhalf(fB, j), OP.mult)),
                 reads=["pp%d" % v_ids[j], "fB%d" % j], writes=["fB%d" % j],
                 tag="kv%d" % b)

    def phase_scan(b):
        for c in range(NCH):
            init = carry[:, c:c + 1] if b > 0 else 0.0
            rd = ["fA%d" % (c // 2), "fB%d" % (c // 2)] + (["carry"] if b > 0 else [])
            p.op("vector", (lambda c=c, init=init: nc.vector.tensor_tensor_scan(
                f3(fC, c), f3(fA, c), f3(fB, c), init, OP.mult, OP.add)),
                 reads=rd, writes=["fC"], tag="scan%d" % b)
        p.op("vector", (lambda: nc.vector.tensor_copy(
            carry[:, :].rearrange("p (c o) -> p c o", o=1),
            chunked(fC)[:, :, L - 1:L])),
             reads=["fC"], writes=["carry"], tag="carryupd%d" % b)

    def phase_yq(b, q_ids):
        for j in range(4):
            p.op("vector", (lambda j=j, q_ids=q_ids: nc.vector.tensor_tensor(
                fhalf(fC, j), fhalf(fC, j), pp[q_ids[j]][:, :], OP.mult)),
                 reads=["pp%d" % q_ids[j], "fC"], writes=["fC"], tag="yq%d" % b)

    def phase_hoist_g(b):
        rr = ["xhf%d" % (b % 2)]
        rhs = lambda kc, b=b: xhat_view(b, kc)
        g_ids = mm_stage("wg", rhs, rr, "g%d" % b)
        for j in range(4):
            p.op("scalar", (lambda j=j, g_ids=g_ids: nc.scalar.activation(
                fhalf(bfP, j), pp[g_ids[j]][:, :], AF.Sigmoid)),
                 reads=["pp%d" % g_ids[j]], writes=["bfP"], tag="sigg%d" % b)
        return g_ids

    def phase_ygy2(b, g_ids):
        for j in range(4):
            p.op("vector", (lambda j=j, g_ids=g_ids: nc.vector.tensor_tensor(
                fhalf(fC, j), fhalf(fC, j), pp[g_ids[j]][:, :], OP.mult)),
                 reads=["pp%d" % g_ids[j], "fC"], writes=["fC"], tag="yg%d" % b)
        p.op("vector", (lambda: nc.vector.tensor_tensor(
            y2b[:, :], fC[:, :], bfP[:, :], OP.mult)),
             reads=["fC", "bfP"], writes=["y2b"], tag="y2_%d" % b)

    def phase_D(b):
        wo_ids = mm_stage("wo", (lambda kc: chunked(y2b)[:, kc, :]), ["y2b"],
                          "wo%d" % b)
        for j in range(4):
            p.op("scalar", (lambda j=j, wo_ids=wo_ids: nc.scalar.copy(
                fhalf(fB, j), pp[wo_ids[j]][:, :])),
                 reads=["pp%d" % wo_ids[j]], writes=["fB%d" % j], tag="woev%d" % b)
        t_ids = transpose_stage(fB, "fB", "Tattn%d" % b, sub=True)
        resid_stage(b, t_ids, "resid1_%d" % b)

    def phase_E1(b):
        ln_stats(b, 2, "E1_%d" % b)
        ln_apply(b, 2, "E1_%d" % b)

    def phase_E2(b):
        v = chunked(xh2_fm)
        pv = chunked(bfP)
        xr = chunked(y2b)
        p.op("vector", (lambda v=v, pv=pv: nc.vector.tensor_copy(
            pv[:, :, 1:L], v[:, :, 0:L - 1])),
             reads=["xh2"], writes=["bfP"], tag="prev%d" % b)
        if b == 0:
            p.op("vector", (lambda pv=pv: nc.vector.memset(pv[:, :, 0:1], 0.0)),
                 reads=[], writes=["bfP"], tag="prev0_%d" % b)
        else:
            p.op("vector", (lambda pv=pv: nc.vector.tensor_copy(
                pv[:, :, 0:1], carry2[:, :].rearrange("p (c o) -> p c o", o=1))),
                 reads=["carry2"], writes=["bfP"], tag="prev0_%d" % b)
        p.op("vector", (lambda v=v: nc.vector.tensor_copy(
            carry2[:, :].rearrange("p (c o) -> p c o", o=1), v[:, :, L - 1:L])),
             reads=["xh2"], writes=["carry2"], tag="carry2_%d" % b)
        for c in range(NCH):
            p.op("vector", (lambda c=c, v=v, xr=xr: nc.vector.tensor_scalar_mul(
                xr[:, c, :], v[:, c, :], mur[:, c:c + 1])),
                 reads=["xh2", "mur"], writes=["y2b"], tag="xr%d" % b)
            p.op("vector", (lambda c=c, pv=pv, xr=xr: nc.vector.scalar_tensor_tensor(
                xr[:, c, :], pv[:, c, :], onemr[:, c:c + 1], xr[:, c, :],
                OP.mult, OP.add)),
                 reads=["bfP", "onemr", "y2b"], writes=["y2b"], tag="xr%d" % b)
        for c in range(NCH):
            p.op("vector", (lambda c=c, v=v: nc.vector.tensor_scalar_mul(
                v[:, c, :], v[:, c, :], muk[:, c:c + 1])),
                 reads=["xh2", "muk"], writes=["xh2"], tag="xk%d" % b)
            p.op("vector", (lambda c=c, pv=pv, v=v: nc.vector.scalar_tensor_tensor(
                v[:, c, :], pv[:, c, :], onemk[:, c:c + 1], v[:, c, :],
                OP.mult, OP.add)),
                 reads=["bfP", "onemk", "xh2"], writes=["xh2"], tag="xk%d" % b)

    def phase_F1(b):
        xr_fn = lambda kc: chunked(y2b)[:, kc, :]
        r_ids = mm_stage("wrT", xr_fn, ["y2b"], "fr%d" % b)
        for j in range(4):
            p.op("scalar", (lambda j=j, r_ids=r_ids: nc.scalar.activation(
                fhalf(fB, j), pp[r_ids[j]][:, :], AF.Sigmoid)),
                 reads=["pp%d" % r_ids[j]], writes=["fB%d" % j], tag="sigr%d" % b)

    def phase_F2(b):
        xk_fn = lambda kc: chunked(xh2_fm)[:, kc, :]
        k_ids = mm_stage("wkT", xk_fn, ["xh2"], "fk%d" % b)
        for j in range(4):
            p.op("scalar", (lambda j=j, k_ids=k_ids: nc.scalar.activation(
                fhalf(y2b, j), pp[k_ids[j]][:, :], AF.Relu)),
                 reads=["pp%d" % k_ids[j]], writes=["y2b"], tag="krelu%d" % b)
        p.op("scalar", (lambda: nc.scalar.activation(
            xh2_fm[:, :], y2b[:, :], AF.Square)),
             reads=["y2b"], writes=["xh2"], tag="ksq%d" % b)

    def phase_F3(b):
        ks_fn = lambda kc: chunked(xh2_fm)[:, kc, :]
        v_ids = mm_stage("wvT", ks_fn, ["xh2"], "fv%d" % b)
        for j in range(4):
            p.op("vector", (lambda j=j, v_ids=v_ids: nc.vector.tensor_tensor(
                fhalf(fB, j), pp[v_ids[j]][:, :], fhalf(fB, j), OP.mult)),
                 reads=["pp%d" % v_ids[j], "fB%d" % j], writes=["fB%d" % j],
                 tag="fvev%d" % b)

    def phase_F4(b):
        t_ids = transpose_stage(fB, "fB", "Tffn%d" % b, sub=True)
        resid_stage(b, t_ids, "resid2_%d" % b)

    def phase_G(b):
        par = b % 3
        dst = out_d[b * L:(b + 1) * L, :].rearrange("(t p) d -> p t d", p=128)
        srcv = x_tm[:, par, :].rearrange("p (t d) -> p t d", t=NTI)
        store_toks.append(
            p.op("sync", (lambda dst=dst, srcv=srcv: nc.sync.dma_start(dst, srcv)),
                 reads=["x%dt0" % par, "x%dt1" % par], writes=[], dma=True,
                 tag="store%d" % b))
        if b + 3 < NB:
            load_x(b + 3, "sync")

    phase_A(0)
    phase_A(1)
    vq = {}
    g_ids_of = {}
    vq[0] = phase_hoist_kvq_mm(0)
    phase_kvev(0, vq[0][0])
    phase_hoist_a(0)
    phase_scan(0)
    phase_yq(0, vq[0][1])
    g_ids_of[0] = phase_hoist_g(0)
    phase_ygy2(0, g_ids_of[0])
    phase_hoist_a(1)
    for b in range(NB):
        phase_D(b)
        phase_E1(b)
        if b + 1 < NB:
            vq[b + 1] = phase_hoist_kvq_mm(b + 1)
        phase_E2(b)
        if b + 2 < NB:
            phase_A(b + 2)
        if b + 1 < NB:
            phase_kvev(b + 1, vq[b + 1][0])
            phase_scan(b + 1)
            phase_yq(b + 1, vq[b + 1][1])
        phase_F1(b)
        phase_F2(b)
        if b + 1 < NB:
            g_ids_of[b + 1] = phase_hoist_g(b + 1)
            phase_ygy2(b + 1, g_ids_of[b + 1])
        phase_F3(b)
        phase_F4(b)
        if b + 2 < NB:
            phase_hoist_a(b + 2)
        phase_G(b)
        if b + 2 < NB:
            phase_A(b + 2)

    p.finalize()

    sems = {}
    sem_guards = []
    sem_names = ["tensor", "vector", "scalar", "gpsimd", "sync"]
    sem_names += ["dq%d" % (i + 1) for i in range(p.counts.get("dq", 0))]
    for name in sem_names:
        g = nc.semaphore("sem_" + name)
        sems[name] = g.__enter__()
        sem_guards.append(g)

    per_engine = {e: [] for e in Plan.ENGINES}
    for rec in p.ops:
        per_engine[rec["engine"]].append(rec)

    final_waits = [(sem, 16) for (sem, _v) in store_toks]

    with nc.Block() as block:
        def run_engine(eng_name, eng):
            for rec in per_engine[eng_name]:
                for (sem, val) in rec["waits"]:
                    eng.wait_ge(sems[sem], val)
                ins = rec["emit"]()
                if rec["inc"]:
                    step = 16 if rec["sem"].startswith("dq") else 1
                    ins.then_inc(sems[rec["sem"]], step)

        @block.tensor
        def _(eng):
            run_engine("tensor", eng)

        @block.vector
        def _(eng):
            run_engine("vector", eng)

        @block.scalar
        def _(eng):
            run_engine("scalar", eng)

        @block.gpsimd
        def _(eng):
            run_engine("gpsimd", eng)

        @block.sync
        def _(eng):
            run_engine("sync", eng)
            for (sem, val) in final_waits:
                eng.wait_ge(sems[sem], val)

    for g in reversed(sem_guards):
        g.__exit__(None, None, None)
    for g in reversed(ctx):
        g.__exit__(None, None, None)
    return nc


def kernel(L0_attn_x, L0_attn_gamma, L0_attn_wq, L0_attn_wk, L0_attn_wv,
           L0_attn_wa, L0_attn_wg, L0_attn_wo, L0_ffn_W_r, L0_ffn_W_k,
           L0_ffn_W_v, L0_ffn_mu_r, L0_ffn_mu_k):
    x = np.ascontiguousarray(np.asarray(L0_attn_x, dtype=np.float32))
    def lay(w):
        return np.ascontiguousarray(
            np.asarray(w, np.float32).reshape(8, 128, 1024).transpose(1, 0, 2))

    com = {
        "gamma": np.ascontiguousarray(
            np.asarray(L0_attn_gamma, np.float32).reshape(8, 128).T),
        "wq": lay(L0_attn_wq),
        "wk": lay(L0_attn_wk),
        "wv": lay(L0_attn_wv),
        "wa": lay(L0_attn_wa),
        "wg": lay(L0_attn_wg),
        "wo": lay(L0_attn_wo),
        "wrT": lay(np.asarray(L0_ffn_W_r, np.float32).T),
        "wkT": lay(np.asarray(L0_ffn_W_k, np.float32).T),
        "wvT": lay(np.asarray(L0_ffn_W_v, np.float32).T),
        "mu_r": np.ascontiguousarray(
            np.asarray(L0_ffn_mu_r, np.float32).reshape(8, 128).T),
        "mu_k": np.ascontiguousarray(
            np.asarray(L0_ffn_mu_k, np.float32).reshape(8, 128).T),
        "ident": np.eye(128, dtype=np.float32),
    }
    nc = build_program()
    in_maps = [dict(com, x=np.ascontiguousarray(x[c])) for c in range(8)]
    res = run_bass_kernel_spmd(nc, in_maps, core_ids=list(range(8)))
    return np.stack([np.asarray(r["out"], dtype=np.float32)
                     for r in res.results], axis=0)


# version 13
# speedup vs baseline: 1.1118x; 1.0024x over previous
import numpy as np
import concourse.bass as bass
import concourse.mybir as mybir
from concourse.bass_utils import run_bass_kernel_spmd

F32 = mybir.dt.float32
BF16 = mybir.dt.bfloat16
AF = mybir.ActivationFunctionType
OP = mybir.AluOpType

D = 1024
T = 2048
L = 256
NB = T // L
NCH = D // 128
NTI = L // 128
EPS = 1e-5


class Plan:

    ENGINES = ("tensor", "vector", "scalar", "gpsimd", "sync")

    def __init__(self):
        self.ops = []
        self.counts = {}
        self.regions = {}
        self.tok_group = {}

    def op(self, engine, emit, reads=(), writes=(), dma=False, tag="",
           group=None):
        if dma:
            self.counts["dq"] = self.counts.get("dq", 0) + 1
            sem = "dq%d" % self.counts["dq"]
            val = 16
        else:
            sem = engine
            val = self.counts.get(sem, 0) + 1
            self.counts[sem] = val
        token = (sem, val)
        deps = set()
        for r in reads:
            reg = self.regions.get(r)
            if reg and reg["w"]:
                deps.add(reg["w"])
            if reg and r.startswith("pp"):
                deps.update(t for t in reg["r"] if t[0] != sem)
        for w in writes:
            reg = self.regions.get(w)
            if reg:
                if reg["w"]:
                    deps.add(reg["w"])
                deps.update(reg["r"])
        if group is not None:
            deps = {d for d in deps if self.tok_group.get(d) != group}
            self.tok_group[token] = group
        self.ops.append(dict(engine=engine, emit=emit, sem=sem, val=val,
                             deps=sorted(deps), dma=dma, tag=tag))
        for r in reads:
            reg = self.regions.setdefault(r, {"w": None, "r": []})
            reg["r"].append(token)
        for w in writes:
            self.regions[w] = {"w": token, "r": []}
        return token

    def finalize(self):
        needed = {}
        for rec in self.ops:
            for (sem, val) in rec["deps"]:
                needed.setdefault(sem, set()).add(val)
        self.needed = {sem: sorted(v) for sem, v in needed.items()}
        self.incval = {}
        for sem, vals in self.needed.items():
            if sem.startswith("dq"):
                continue
            for i, v in enumerate(vals):
                self.incval[(sem, v)] = i + 1
        known = {}
        for rec in self.ops:
            w = []
            for (sem, val) in sorted(rec["deps"]):
                if sem.startswith("dq"):
                    cnt, wv = 16, 16
                else:
                    cnt = self.incval[(sem, val)]
                    wv = cnt
                k = known.setdefault(rec["engine"], {})
                if k.get(sem, 0) < cnt:
                    k[sem] = cnt
                    w.append((sem, wv))
            rec["waits"] = w
            if rec["sem"].startswith("dq"):
                rec["inc"] = True
            else:
                rec["inc"] = rec["val"] in set(self.needed.get(rec["sem"], []))


def build_program():
    nc = bass.Bass("TRN2", target_bir_lowering=False,
                   dynamic_dma_scratch_size=4096)

    x_d = nc.dram_tensor("x", [T, D], F32, kind="ExternalInput")
    gam_d = nc.dram_tensor("gamma", [128, NCH], F32, kind="ExternalInput")
    w_d = {}
    for n in ("wq", "wk", "wv", "wa", "wg", "wo", "wrT", "wkT", "wvT"):
        w_d[n] = nc.dram_tensor(n, [128, NCH, D], F32, kind="ExternalInput")
    mur_d = nc.dram_tensor("mu_r", [128, NCH], F32, kind="ExternalInput")
    muk_d = nc.dram_tensor("mu_k", [128, NCH], F32, kind="ExternalInput")
    id_d = nc.dram_tensor("ident", [128, 128], F32, kind="ExternalInput")
    out_d = nc.dram_tensor("out", [T, D], F32, kind="ExternalOutput")

    ctx = []

    def sb(name, shape, dtype):
        g = nc.sbuf_tensor(name, shape, dtype)
        t = g.__enter__()
        ctx.append(g)
        return t

    def psum(name, shape, dtype):
        g = nc.psum_tensor(name, shape, dtype)
        t = g.__enter__()
        ctx.append(g)
        return t

    wsb = {}
    for n in ("wq", "wk", "wv", "wa", "wg", "wo", "wrT", "wkT", "wvT"):
        wsb[n] = sb("W_" + n, [128, NCH, D], BF16)
    x_tm = sb("x_tm", [128, 3, NTI * D], F32)
    xhat_fm = sb("xhat_fm", [128, 2, NCH * L], BF16)
    xh2_fm = sb("xh2_fm", [128, NCH * L], BF16)
    bfP = sb("bfP", [128, NCH * L], BF16)
    stg1 = sb("stg1", [128, NTI * D], BF16)
    y2b = sb("y2b", [128, NCH * L], BF16)
    fA = sb("fA", [128, NCH * L], F32)
    fB = sb("fB", [128, NCH * L], F32)
    fC = sb("fC", [128, NCH * L], F32)
    stats = sb("stats", [128, 36], F32)
    gam = sb("gam", [128, NCH], F32)
    mur = sb("mur", [128, NCH], F32)
    muk = sb("muk", [128, NCH], F32)
    onemr = sb("onemr", [128, NCH], F32)
    onemk = sb("onemk", [128, NCH], F32)
    carry = sb("carry", [128, NCH], F32)
    carry2 = sb("carry2", [128, NCH], BF16)
    ident = sb("identity", [128, 128], F32)

    pp = [psum("pp%d" % i, [128, 512], F32) for i in range(8)]

    p = Plan()

    def wview(n, kc, dlo, dhi):
        return wsb[n][:, kc, dlo:dhi]

    def xhat_view(b, c):
        return xhat_fm[:, b % 2, c * L:(c + 1) * L]

    def f3(t, c):
        return t[:, c * L:(c + 1) * L]

    def fhalf(t, j):
        return t[:, j * 512:(j + 1) * 512]

    def chunked(t):
        return t[:, :].rearrange("p (c l) -> p c l", c=NCH)

    BN = 0
    MV1, MV2 = 12, 16
    SS1, SS2 = 20, 22
    CU, CT, CR, CQ1 = 24, 25, 26, 27

    ppi = [0]

    def next_pps():
        base = ppi[0]
        ids = [(base + i) % 8 for i in range(4)]
        ppi[0] = (base + 4) % 8
        return ids

    def mm_stage(wname, rhs_fn, rhs_regions, tag):
        ids = next_pps()
        for j in range(4):
            pj = pp[ids[j]]
            for h in range(2):
                douc = 2 * j + h
                for kc in range(NCH):
                    p.op("tensor",
                         (lambda pj=pj, h=h, wname=wname, kc=kc, douc=douc,
                                 st=(kc == 0), sp=(kc == NCH - 1), rhs_fn=rhs_fn:
                          nc.tensor.matmul(pj[:, h * 256:(h + 1) * 256],
                                           wview(wname, kc, douc * 128, (douc + 1) * 128),
                                           rhs_fn(kc), start=st, stop=sp)),
                         reads=["W_" + wname] + list(rhs_regions),
                         writes=["pp%d" % ids[j]], tag=tag,
                         group=tag + "_p%d" % ids[j])
        return ids

    def transpose_stage(srcbuf, srcreg, tag, sub=False):
        ids = next_pps()
        for c in range(NCH):
            for ti in range(NTI):
                pj = pp[ids[2 * ti + c // 4]]
                src = chunked(srcbuf)[:, c, ti * 128:(ti + 1) * 128]
                rd = [("%s%d" % (srcreg, c // 2)) if sub else srcreg, "ident"]
                p.op("tensor", (lambda pj=pj, src=src, c=c: nc.tensor.transpose(
                    pj[:, (c % 4) * 128:(c % 4 + 1) * 128], src, ident[:, :])),
                     reads=rd,
                     writes=["pp%d" % ids[2 * ti + c // 4]], tag=tag,
                     group=tag + "_p%d" % ids[2 * ti + c // 4])
        return ids

    def resid_stage(b, t_ids, tag):
        par = b % 3
        for ti in range(NTI):
            for hh in range(2):
                j = 2 * ti + hh
                xv = x_tm[:, par, ti * D + hh * 512: ti * D + (hh + 1) * 512]
                xrg = "x%dt%d" % (par, ti)
                p.op("vector", (lambda xv=xv, j=j, t_ids=t_ids:
                                nc.vector.tensor_tensor(xv, xv, pp[t_ids[j]][:, :], OP.add)),
                     reads=["pp%d" % t_ids[j], xrg], writes=[xrg],
                     tag=tag)

    MV1, MV2, SS1, SS2 = 12, 16, 20, 22
    BB1, BB2 = 32, 34
    CUc, CTc, CRc, CQ1c = 24, 26, 28, 30

    def ln_stats(b, which, tag):
        par = b % 3
        base_mv = MV1 if which == 1 else MV2
        ssc = SS1 if which == 1 else SS2
        xreg = "x%d" % par
        reg = "stats%d" % which
        for ti in range(NTI):
            xt = x_tm[:, par, ti * D:(ti + 1) * D]
            xrg = "%st%d" % (xreg, ti)
            p.op("vector", (lambda xt=xt: nc.vector.bn_stats(
                stats[:, 0:6], xt[:, 0:512])),
                 reads=[xrg], writes=["statsB"], tag=tag)
            p.op("vector", (lambda xt=xt: nc.vector.bn_stats(
                stats[:, 6:12], xt[:, 512:1024])),
                 reads=[xrg], writes=["statsB"], tag=tag)
            p.op("vector", (lambda base_mv=base_mv, ti=ti: nc.vector.bn_aggr(
                stats[:, base_mv + 2 * ti:base_mv + 2 * ti + 2], stats[:, 0:12])),
                 reads=["statsB"], writes=[reg, "statsB"], tag=tag)
        vv = stats[:, base_mv + 1:base_mv + 4:2]
        uu = stats[:, CUc:CUc + 2]
        if which == 1:
            p.op("vector", (lambda vv=vv, uu=uu: nc.vector.tensor_scalar(
                uu, vv, (D + EPS) / D, (EPS * EPS) / D, OP.mult, OP.add)),
                 reads=[reg], writes=["statsN"], tag=tag)
        else:
            p.op("vector", (lambda vv=vv, uu=uu: nc.vector.tensor_scalar(
                uu, vv, 1.0, EPS, OP.mult, OP.add)),
                 reads=[reg], writes=["statsN"], tag=tag)
        p.op("scalar", (lambda uu=uu: nc.scalar.activation(
            stats[:, CTc:CTc + 2], uu, AF.Sqrt)),
             reads=["statsN"], writes=["statsN"], tag=tag + "_sqrt")
        p.op("vector", (lambda ssc=ssc: nc.vector.reciprocal(
            stats[:, ssc:ssc + 2], stats[:, CTc:CTc + 2])),
             reads=["statsN"], writes=[reg, "statsN"], tag=tag)
        bbc = BB1 if which == 1 else BB2
        mm_ap = stats[:, base_mv:base_mv + 4:2]
        p.op("vector", (lambda ssc=ssc, bbc=bbc, mm_ap=mm_ap:
                        nc.vector.scalar_tensor_tensor(
                            stats[:, bbc:bbc + 2], mm_ap, -1.0,
                            stats[:, ssc:ssc + 2], OP.mult, OP.mult)),
             reads=[reg], writes=[reg], tag=tag)

    def ln_apply(b, which, tag):
        par = b % 3
        ssc = SS1 if which == 1 else SS2
        xreg = "x%d" % par
        reg = "stats%d" % which
        dstreg = ("xhf%d" % (b % 2)) if which == 1 else "xh2"
        bbc = BB1 if which == 1 else BB2
        stg = stg1 if which == 1 else bfP
        streg = "stg1" if which == 1 else "bfP"
        for ti in range(NTI):
            b_ap = stats[:, bbc + ti:bbc + ti + 1]
            m_ap = stats[:, (MV1 if which == 1 else MV2) + 2 * ti:
                         (MV1 if which == 1 else MV2) + 2 * ti + 1]
            s_ap = stats[:, ssc + ti:ssc + ti + 1]
            xt = x_tm[:, par, ti * D:(ti + 1) * D]
            if ti == 0:
                p.op("vector", (lambda xt=xt, m_ap=m_ap, s_ap=s_ap, stg=stg:
                                nc.vector.tensor_scalar(stg[:, 0:D], xt, m_ap, s_ap,
                                                        OP.subtract, OP.mult)),
                     reads=[xreg + "t0", reg], writes=[streg], tag=tag)
            else:
                p.op("scalar", (lambda xt=xt, b_ap=b_ap, s_ap=s_ap, stg=stg:
                                nc.scalar.activation(stg[:, D:2 * D], xt,
                                                     AF.Identity, bias=b_ap, scale=s_ap)),
                     reads=[xreg + "t1", reg], writes=[streg], tag=tag)
        for ti in range(NTI):
            if which == 1:
                dst = xhat_fm[:, b % 2, :].rearrange(
                    "p (c l) -> p c l", c=NCH)[:, :, ti * 128:(ti + 1) * 128]
            else:
                dst = chunked(xh2_fm)[:, :, ti * 128:(ti + 1) * 128]
            p.op("sync", (lambda dst=dst, ti=ti, stg=stg:
                          nc.sync.dma_start_transpose(
                              dst, stg[:, ti * D:(ti + 1) * D])),
                 reads=[streg], writes=[dstreg], dma=True, tag=tag + "_dmaT")

    def load_x(b, eng):
        par = b % 3
        for ti in range(NTI):
            srcap = x_d[b * L + ti * 128: b * L + (ti + 1) * 128, :]
            dst = x_tm[:, par, ti * D:(ti + 1) * D]
            p.op(eng, (lambda dst=dst, srcap=srcap, eng=eng:
                       getattr(nc, eng).dma_start(dst, srcap)),
                 reads=[], writes=["x%dt%d" % (par, ti)], dma=True,
                 tag="load_x%d" % b)

    def small_load(dst, src, reg):
        p.op("sync", (lambda dst=dst, src=src: nc.sync.dma_start(dst, src)),
             reads=[], writes=[reg], dma=True, tag="prep_small")

    load_x(0, "sync")
    load_x(1, "sync")
    load_x(2, "sync")
    small_load(gam[:, :], gam_d[:, :], "gam")
    small_load(mur[:, :], mur_d[:, :], "mur")
    small_load(muk[:, :], muk_d[:, :], "muk")
    small_load(ident[:, :], id_d[:, :], "ident")

    for n in ("wk", "wv", "wa", "wq", "wg", "wo", "wrT", "wkT", "wvT"):
        p.op("gpsimd", (lambda n=n:
                        nc.gpsimd.dma_start(wsb[n][:, :, :], w_d[n][:, :, :])),
             reads=[], writes=["W_" + n], dma=True, tag="cast_" + n)
        if n in ("wk", "wv", "wa", "wq", "wg"):
            for c in range(NCH):
                p.op("vector", (lambda n=n, c=c: nc.vector.tensor_scalar_mul(
                    wsb[n][:, c, :], wsb[n][:, c, :], gam[:, c:c + 1])),
                     reads=["gam", "W_" + n], writes=["W_" + n], tag="gfold_" + n)

    p.op("vector", (lambda: nc.vector.tensor_scalar(
        onemr[:, :], mur[:, :], -1.0, 1.0, OP.mult, OP.add)),
         reads=["mur"], writes=["onemr"], tag="onemr")

    p.op("vector", (lambda: nc.vector.tensor_scalar(
        onemk[:, :], muk[:, :], -1.0, 1.0, OP.mult, OP.add)),
         reads=["muk"], writes=["onemk"], tag="onemk")

    store_toks = []

    def phase_A(b):
        ln_stats(b, 1, "A%d" % b)
        ln_apply(b, 1, "A%d" % b)

    def phase_hoist_a(b):
        rr = ["xhf%d" % (b % 2)]
        rhs = lambda kc, b=b: xhat_view(b, kc)
        a_ids = mm_stage("wa", rhs, rr, "a%d" % b)
        for j in range(4):
            p.op("scalar", (lambda j=j, a_ids=a_ids: nc.scalar.activation(
                fhalf(fA, j), pp[a_ids[j]][:, :], AF.Sigmoid)),
                 reads=["pp%d" % a_ids[j]], writes=["fA%d" % j], tag="siga%d" % b)

    def phase_hoist_kvq_mm(b):
        rr = ["xhf%d" % (b % 2)]
        rhs = lambda kc, b=b: xhat_view(b, kc)
        k_ids = mm_stage("wk", rhs, rr, "k%d" % b)
        for j in range(4):
            p.op("scalar", (lambda j=j, k_ids=k_ids: nc.scalar.copy(
                fhalf(fB, j), pp[k_ids[j]][:, :])),
                 reads=["pp%d" % k_ids[j]], writes=["fB%d" % j], tag="kev%d" % b)
        v_ids = mm_stage("wv", rhs, rr, "v%d" % b)
        q_ids = mm_stage("wq", rhs, rr, "q%d" % b)
        return v_ids, q_ids

    def phase_kvev(b, v_ids):
        for j in range(4):
            p.op("vector", (lambda j=j, v_ids=v_ids: nc.vector.tensor_tensor(
                fhalf(fB, j), pp[v_ids[j]][:, :], fhalf(fB, j), OP.mult)),
                 reads=["pp%d" % v_ids[j], "fB%d" % j], writes=["fB%d" % j],
                 tag="kv%d" % b)

    def phase_scan(b):
        for c in range(NCH):
            init = carry[:, c:c + 1] if b > 0 else 0.0
            rd = ["fA%d" % (c // 2), "fB%d" % (c // 2)] + (["carry"] if b > 0 else [])
            p.op("vector", (lambda c=c, init=init: nc.vector.tensor_tensor_scan(
                f3(fC, c), f3(fA, c), f3(fB, c), init, OP.mult, OP.add)),
                 reads=rd, writes=["fC"], tag="scan%d" % b)
        p.op("vector", (lambda: nc.vector.tensor_copy(
            carry[:, :].rearrange("p (c o) -> p c o", o=1),
            chunked(fC)[:, :, L - 1:L])),
             reads=["fC"], writes=["carry"], tag="carryupd%d" % b)

    def phase_yq(b, q_ids):
        for j in range(4):
            p.op("vector", (lambda j=j, q_ids=q_ids: nc.vector.tensor_tensor(
                fhalf(fC, j), fhalf(fC, j), pp[q_ids[j]][:, :], OP.mult)),
                 reads=["pp%d" % q_ids[j], "fC"], writes=["fC"], tag="yq%d" % b)

    def phase_hoist_g(b):
        rr = ["xhf%d" % (b % 2)]
        rhs = lambda kc, b=b: xhat_view(b, kc)
        g_ids = mm_stage("wg", rhs, rr, "g%d" % b)
        for j in range(4):
            p.op("scalar", (lambda j=j, g_ids=g_ids: nc.scalar.activation(
                fhalf(bfP, j), pp[g_ids[j]][:, :], AF.Sigmoid)),
                 reads=["pp%d" % g_ids[j]], writes=["bfP"], tag="sigg%d" % b)
        return g_ids

    def phase_ygy2(b, g_ids):
        for j in range(4):
            p.op("vector", (lambda j=j, g_ids=g_ids: nc.vector.tensor_tensor(
                fhalf(fC, j), fhalf(fC, j), pp[g_ids[j]][:, :], OP.mult)),
                 reads=["pp%d" % g_ids[j], "fC"], writes=["fC"], tag="yg%d" % b)
        p.op("vector", (lambda: nc.vector.tensor_tensor(
            y2b[:, :], fC[:, :], bfP[:, :], OP.mult)),
             reads=["fC", "bfP"], writes=["y2b"], tag="y2_%d" % b)

    def phase_D(b):
        wo_ids = mm_stage("wo", (lambda kc: chunked(y2b)[:, kc, :]), ["y2b"],
                          "wo%d" % b)
        for j in range(4):
            p.op("scalar", (lambda j=j, wo_ids=wo_ids: nc.scalar.copy(
                fhalf(fB, j), pp[wo_ids[j]][:, :])),
                 reads=["pp%d" % wo_ids[j]], writes=["fB%d" % j], tag="woev%d" % b)
        t_ids = transpose_stage(fB, "fB", "Tattn%d" % b, sub=True)
        resid_stage(b, t_ids, "resid1_%d" % b)

    def phase_E1(b):
        ln_stats(b, 2, "E1_%d" % b)
        ln_apply(b, 2, "E1_%d" % b)

    def phase_E2(b):
        v = chunked(xh2_fm)
        pv = chunked(bfP)
        xr = chunked(y2b)
        p.op("vector", (lambda v=v, pv=pv: nc.vector.tensor_copy(
            pv[:, :, 1:L], v[:, :, 0:L - 1])),
             reads=["xh2"], writes=["bfP"], tag="prev%d" % b)
        if b == 0:
            p.op("vector", (lambda pv=pv: nc.vector.memset(pv[:, :, 0:1], 0.0)),
                 reads=[], writes=["bfP"], tag="prev0_%d" % b)
        else:
            p.op("vector", (lambda pv=pv: nc.vector.tensor_copy(
                pv[:, :, 0:1], carry2[:, :].rearrange("p (c o) -> p c o", o=1))),
                 reads=["carry2"], writes=["bfP"], tag="prev0_%d" % b)
        p.op("vector", (lambda v=v: nc.vector.tensor_copy(
            carry2[:, :].rearrange("p (c o) -> p c o", o=1), v[:, :, L - 1:L])),
             reads=["xh2"], writes=["carry2"], tag="carry2_%d" % b)
        tsh = chunked(stg1)
        p.op("vector", (lambda: nc.vector.tensor_sub(
            stg1[:, :], xh2_fm[:, :], bfP[:, :])),
             reads=["xh2", "bfP"], writes=["stg1"], tag="xr%d" % b)
        for c in range(NCH):
            p.op("vector", (lambda c=c, tsh=tsh, pv=pv, xr=xr:
                            nc.vector.scalar_tensor_tensor(
                                xr[:, c, :], tsh[:, c, :], mur[:, c:c + 1],
                                pv[:, c, :], OP.mult, OP.add)),
                 reads=["stg1", "mur", "bfP"], writes=["y2b"], tag="xr%d" % b)
        for c in range(NCH):
            p.op("vector", (lambda c=c, tsh=tsh, pv=pv, v=v:
                            nc.vector.scalar_tensor_tensor(
                                v[:, c, :], tsh[:, c, :], muk[:, c:c + 1],
                                pv[:, c, :], OP.mult, OP.add)),
                 reads=["stg1", "muk", "bfP"], writes=["xh2"], tag="xk%d" % b)

    def phase_F1(b):
        xr_fn = lambda kc: chunked(y2b)[:, kc, :]
        r_ids = mm_stage("wrT", xr_fn, ["y2b"], "fr%d" % b)
        for j in range(4):
            p.op("scalar", (lambda j=j, r_ids=r_ids: nc.scalar.activation(
                fhalf(fB, j), pp[r_ids[j]][:, :], AF.Sigmoid)),
                 reads=["pp%d" % r_ids[j]], writes=["fB%d" % j], tag="sigr%d" % b)

    def phase_F2(b):
        xk_fn = lambda kc: chunked(xh2_fm)[:, kc, :]
        k_ids = mm_stage("wkT", xk_fn, ["xh2"], "fk%d" % b)
        for j in range(4):
            p.op("scalar", (lambda j=j, k_ids=k_ids: nc.scalar.activation(
                fhalf(y2b, j), pp[k_ids[j]][:, :], AF.Relu)),
                 reads=["pp%d" % k_ids[j]], writes=["y2b"], tag="krelu%d" % b)
        p.op("scalar", (lambda: nc.scalar.activation(
            xh2_fm[:, :], y2b[:, :], AF.Square)),
             reads=["y2b"], writes=["xh2"], tag="ksq%d" % b)

    def phase_F3(b):
        ks_fn = lambda kc: chunked(xh2_fm)[:, kc, :]
        v_ids = mm_stage("wvT", ks_fn, ["xh2"], "fv%d" % b)
        for j in range(4):
            p.op("vector", (lambda j=j, v_ids=v_ids: nc.vector.tensor_tensor(
                fhalf(fB, j), pp[v_ids[j]][:, :], fhalf(fB, j), OP.mult)),
                 reads=["pp%d" % v_ids[j], "fB%d" % j], writes=["fB%d" % j],
                 tag="fvev%d" % b)

    def phase_F4(b):
        t_ids = transpose_stage(fB, "fB", "Tffn%d" % b, sub=True)
        resid_stage(b, t_ids, "resid2_%d" % b)

    def phase_G(b):
        par = b % 3
        dst = out_d[b * L:(b + 1) * L, :].rearrange("(t p) d -> p t d", p=128)
        srcv = x_tm[:, par, :].rearrange("p (t d) -> p t d", t=NTI)
        store_toks.append(
            p.op("sync", (lambda dst=dst, srcv=srcv: nc.sync.dma_start(dst, srcv)),
                 reads=["x%dt0" % par, "x%dt1" % par], writes=[], dma=True,
                 tag="store%d" % b))
        if b + 3 < NB:
            load_x(b + 3, "sync")

    phase_A(0)
    phase_A(1)
    vq = {}
    g_ids_of = {}
    vq[0] = phase_hoist_kvq_mm(0)
    phase_kvev(0, vq[0][0])
    phase_hoist_a(0)
    phase_scan(0)
    phase_yq(0, vq[0][1])
    g_ids_of[0] = phase_hoist_g(0)
    phase_ygy2(0, g_ids_of[0])
    phase_hoist_a(1)
    for b in range(NB):
        phase_D(b)
        phase_E1(b)
        if b + 1 < NB:
            vq[b + 1] = phase_hoist_kvq_mm(b + 1)
        phase_E2(b)
        if b + 2 < NB:
            phase_A(b + 2)
        if b + 1 < NB:
            phase_kvev(b + 1, vq[b + 1][0])
            phase_scan(b + 1)
            phase_yq(b + 1, vq[b + 1][1])
        phase_F1(b)
        phase_F2(b)
        if b + 1 < NB:
            g_ids_of[b + 1] = phase_hoist_g(b + 1)
            phase_ygy2(b + 1, g_ids_of[b + 1])
        if b + 2 < NB:
            phase_hoist_a(b + 2)
        phase_F3(b)
        phase_F4(b)
        phase_G(b)
        if b + 2 < NB:
            phase_A(b + 2)

    p.finalize()

    sems = {}
    sem_guards = []
    sem_names = ["tensor", "vector", "scalar", "gpsimd", "sync"]
    sem_names += ["dq%d" % (i + 1) for i in range(p.counts.get("dq", 0))]
    for name in sem_names:
        g = nc.semaphore("sem_" + name)
        sems[name] = g.__enter__()
        sem_guards.append(g)

    per_engine = {e: [] for e in Plan.ENGINES}
    for rec in p.ops:
        per_engine[rec["engine"]].append(rec)

    final_waits = [(sem, 16) for (sem, _v) in store_toks]

    with nc.Block() as block:
        def run_engine(eng_name, eng):
            for rec in per_engine[eng_name]:
                for (sem, val) in rec["waits"]:
                    eng.wait_ge(sems[sem], val)
                ins = rec["emit"]()
                if rec["inc"]:
                    step = 16 if rec["sem"].startswith("dq") else 1
                    ins.then_inc(sems[rec["sem"]], step)

        @block.tensor
        def _(eng):
            run_engine("tensor", eng)

        @block.vector
        def _(eng):
            run_engine("vector", eng)

        @block.scalar
        def _(eng):
            run_engine("scalar", eng)

        @block.gpsimd
        def _(eng):
            run_engine("gpsimd", eng)

        @block.sync
        def _(eng):
            run_engine("sync", eng)
            for (sem, val) in final_waits:
                eng.wait_ge(sems[sem], val)

    for g in reversed(sem_guards):
        g.__exit__(None, None, None)
    for g in reversed(ctx):
        g.__exit__(None, None, None)
    return nc


def kernel(L0_attn_x, L0_attn_gamma, L0_attn_wq, L0_attn_wk, L0_attn_wv,
           L0_attn_wa, L0_attn_wg, L0_attn_wo, L0_ffn_W_r, L0_ffn_W_k,
           L0_ffn_W_v, L0_ffn_mu_r, L0_ffn_mu_k):
    x = np.ascontiguousarray(np.asarray(L0_attn_x, dtype=np.float32))
    def lay(w):
        return np.ascontiguousarray(
            np.asarray(w, np.float32).reshape(8, 128, 1024).transpose(1, 0, 2))

    com = {
        "gamma": np.ascontiguousarray(
            np.asarray(L0_attn_gamma, np.float32).reshape(8, 128).T),
        "wq": lay(L0_attn_wq),
        "wk": lay(L0_attn_wk),
        "wv": lay(L0_attn_wv),
        "wa": lay(L0_attn_wa),
        "wg": lay(L0_attn_wg),
        "wo": lay(L0_attn_wo),
        "wrT": lay(np.asarray(L0_ffn_W_r, np.float32).T),
        "wkT": lay(np.asarray(L0_ffn_W_k, np.float32).T),
        "wvT": lay(np.asarray(L0_ffn_W_v, np.float32).T),
        "mu_r": np.ascontiguousarray(
            np.asarray(L0_ffn_mu_r, np.float32).reshape(8, 128).T),
        "mu_k": np.ascontiguousarray(
            np.asarray(L0_ffn_mu_k, np.float32).reshape(8, 128).T),
        "ident": np.eye(128, dtype=np.float32),
    }
    nc = build_program()
    in_maps = [dict(com, x=np.ascontiguousarray(x[c])) for c in range(8)]
    res = run_bass_kernel_spmd(nc, in_maps, core_ids=list(range(8)))
    return np.stack([np.asarray(r["out"], dtype=np.float32)
                     for r in res.results], axis=0)


# version 14
# speedup vs baseline: 1.1969x; 1.0765x over previous
import numpy as np
import concourse.bass as bass
import concourse.mybir as mybir
from concourse.bass_utils import run_bass_kernel_spmd

F32 = mybir.dt.float32
BF16 = mybir.dt.bfloat16
AF = mybir.ActivationFunctionType
OP = mybir.AluOpType

D = 1024
T = 2048
L = 256
NB = T // L
NCH = D // 128
NTI = L // 128
EPS = 1e-5


class Plan:

    ENGINES = ("tensor", "vector", "scalar", "gpsimd", "sync")

    def __init__(self):
        self.ops = []
        self.counts = {}
        self.regions = {}
        self.tok_group = {}

    def op(self, engine, emit, reads=(), writes=(), dma=False, tag="",
           group=None):
        if dma:
            self.counts["dq"] = self.counts.get("dq", 0) + 1
            sem = "dq%d" % self.counts["dq"]
            val = 16
        else:
            sem = engine
            val = self.counts.get(sem, 0) + 1
            self.counts[sem] = val
        token = (sem, val)
        deps = set()
        for r in reads:
            reg = self.regions.get(r)
            if reg and reg["w"]:
                deps.add(reg["w"])
            if reg and r.startswith("pp"):
                deps.update(t for t in reg["r"] if t[0] != sem)
        for w in writes:
            reg = self.regions.get(w)
            if reg:
                if reg["w"]:
                    deps.add(reg["w"])
                deps.update(reg["r"])
        if group is not None:
            deps = {d for d in deps if self.tok_group.get(d) != group}
            self.tok_group[token] = group
        self.ops.append(dict(engine=engine, emit=emit, sem=sem, val=val,
                             deps=sorted(deps), dma=dma, tag=tag))
        for r in reads:
            reg = self.regions.setdefault(r, {"w": None, "r": []})
            reg["r"].append(token)
        for w in writes:
            self.regions[w] = {"w": token, "r": []}
        return token

    def finalize(self):
        needed = {}
        for rec in self.ops:
            for (sem, val) in rec["deps"]:
                needed.setdefault(sem, set()).add(val)
        self.needed = {sem: sorted(v) for sem, v in needed.items()}
        self.incval = {}
        for sem, vals in self.needed.items():
            if sem.startswith("dq"):
                continue
            for i, v in enumerate(vals):
                self.incval[(sem, v)] = i + 1
        known = {}
        for rec in self.ops:
            w = []
            for (sem, val) in sorted(rec["deps"]):
                if sem.startswith("dq"):
                    cnt, wv = 16, 16
                else:
                    cnt = self.incval[(sem, val)]
                    wv = cnt
                k = known.setdefault(rec["engine"], {})
                if k.get(sem, 0) < cnt:
                    k[sem] = cnt
                    w.append((sem, wv))
            rec["waits"] = w
            if rec["sem"].startswith("dq"):
                rec["inc"] = True
            else:
                rec["inc"] = rec["val"] in set(self.needed.get(rec["sem"], []))


def build_program():
    nc = bass.Bass("TRN2", target_bir_lowering=False,
                   dynamic_dma_scratch_size=4096)

    x_d = nc.dram_tensor("x", [T, D], F32, kind="ExternalInput")
    gam_d = nc.dram_tensor("gamma", [128, NCH], F32, kind="ExternalInput")
    w_d = {}
    for n in ("wq", "wk", "wv", "wa", "wg", "wo", "wrT", "wkT", "wvT"):
        w_d[n] = nc.dram_tensor(n, [128, NCH, D], F32, kind="ExternalInput")
    mur_d = nc.dram_tensor("mu_r", [128, NCH], F32, kind="ExternalInput")
    muk_d = nc.dram_tensor("mu_k", [128, NCH], F32, kind="ExternalInput")
    id_d = nc.dram_tensor("ident", [128, 128], F32, kind="ExternalInput")
    out_d = nc.dram_tensor("out", [T, D], F32, kind="ExternalOutput")

    ctx = []

    def sb(name, shape, dtype):
        g = nc.sbuf_tensor(name, shape, dtype)
        t = g.__enter__()
        ctx.append(g)
        return t

    def psum(name, shape, dtype):
        g = nc.psum_tensor(name, shape, dtype)
        t = g.__enter__()
        ctx.append(g)
        return t

    wsb = {}
    for n in ("wq", "wk", "wv", "wa", "wg", "wo", "wrT", "wkT", "wvT"):
        wsb[n] = sb("W_" + n, [128, NCH, D], BF16)
    x_tm = sb("x_tm", [128, 3, NTI * D], F32)
    xhat_fm = sb("xhat_fm", [128, 2, NCH * L], BF16)
    xh2_fm = sb("xh2_fm", [128, NCH * L], BF16)
    bfP = sb("bfP", [128, NCH * L], BF16)
    stg1 = sb("stg1", [128, NTI * D], BF16)
    y2b = sb("y2b", [128, NCH * L], BF16)
    fA = sb("fA", [128, NCH * L], F32)
    fB = sb("fB", [128, NCH * L], F32)
    fC = sb("fC", [128, NCH * L], F32)
    stats = sb("stats", [128, 36], F32)
    gam = sb("gam", [128, NCH], F32)
    mur = sb("mur", [128, NCH], F32)
    muk = sb("muk", [128, NCH], F32)
    onemr = sb("onemr", [128, NCH], F32)
    onemk = sb("onemk", [128, NCH], F32)
    carry = sb("carry", [128, NCH], F32)
    carry2 = sb("carry2", [128, NCH], BF16)
    ident = sb("identity", [128, 128], F32)

    pp = [psum("pp%d" % i, [128, 512], F32) for i in range(8)]

    p = Plan()

    def wview(n, kc, dlo, dhi):
        return wsb[n][:, kc, dlo:dhi]

    def xhat_view(b, c):
        return xhat_fm[:, b % 2, c * L:(c + 1) * L]

    def f3(t, c):
        return t[:, c * L:(c + 1) * L]

    def fhalf(t, j):
        return t[:, j * 512:(j + 1) * 512]

    def chunked(t):
        return t[:, :].rearrange("p (c l) -> p c l", c=NCH)

    BN = 0
    MV1, MV2 = 12, 16
    SS1, SS2 = 20, 22
    CU, CT, CR, CQ1 = 24, 25, 26, 27

    ppi = [0]

    def next_pps():
        base = ppi[0]
        ids = [(base + i) % 8 for i in range(4)]
        ppi[0] = (base + 4) % 8
        return ids

    def mm_stage(wname, rhs_fn, rhs_regions, tag):
        ids = next_pps()
        for j in range(4):
            pj = pp[ids[j]]
            for h in range(2):
                douc = 2 * j + h
                for kc in range(NCH):
                    p.op("tensor",
                         (lambda pj=pj, h=h, wname=wname, kc=kc, douc=douc,
                                 st=(kc == 0), sp=(kc == NCH - 1), rhs_fn=rhs_fn:
                          nc.tensor.matmul(pj[:, h * 256:(h + 1) * 256],
                                           wview(wname, kc, douc * 128, (douc + 1) * 128),
                                           rhs_fn(kc), start=st, stop=sp)),
                         reads=["W_" + wname] + list(rhs_regions),
                         writes=["pp%d" % ids[j]], tag=tag,
                         group=tag + "_p%d" % ids[j])
        return ids

    def transpose_stage(srcbuf, srcreg, tag, sub=False):
        ids = next_pps()
        for c in range(NCH):
            for ti in range(NTI):
                pj = pp[ids[2 * ti + c // 4]]
                src = chunked(srcbuf)[:, c, ti * 128:(ti + 1) * 128]
                rd = [("%s%d" % (srcreg, c // 2)) if sub else srcreg, "ident"]
                p.op("tensor", (lambda pj=pj, src=src, c=c: nc.tensor.transpose(
                    pj[:, (c % 4) * 128:(c % 4 + 1) * 128], src, ident[:, :])),
                     reads=rd,
                     writes=["pp%d" % ids[2 * ti + c // 4]], tag=tag,
                     group=tag + "_p%d" % ids[2 * ti + c // 4])
        return ids

    def resid_stage(b, t_ids, tag):
        par = b % 3
        for ti in range(NTI):
            for hh in range(2):
                j = 2 * ti + hh
                xv = x_tm[:, par, ti * D + hh * 512: ti * D + (hh + 1) * 512]
                xrg = "x%dt%d" % (par, ti)
                p.op("vector", (lambda xv=xv, j=j, t_ids=t_ids:
                                nc.vector.tensor_tensor(xv, xv, pp[t_ids[j]][:, :], OP.add)),
                     reads=["pp%d" % t_ids[j], xrg], writes=[xrg],
                     tag=tag)

    MV1, MV2, SS1, SS2 = 12, 16, 20, 22
    BB1, BB2 = 32, 34
    CUc, CTc, CRc, CQ1c = 24, 26, 28, 30

    def ln_stats(b, which, tag):
        par = b % 3
        base_mv = MV1 if which == 1 else MV2
        ssc = SS1 if which == 1 else SS2
        xreg = "x%d" % par
        reg = "stats%d" % which
        for ti in range(NTI):
            xt = x_tm[:, par, ti * D:(ti + 1) * D]
            xrg = "%st%d" % (xreg, ti)
            p.op("vector", (lambda xt=xt: nc.vector.bn_stats(
                stats[:, 0:6], xt[:, 0:512])),
                 reads=[xrg], writes=["statsB"], tag=tag)
            p.op("vector", (lambda xt=xt: nc.vector.bn_stats(
                stats[:, 6:12], xt[:, 512:1024])),
                 reads=[xrg], writes=["statsB"], tag=tag)
            p.op("vector", (lambda base_mv=base_mv, ti=ti: nc.vector.bn_aggr(
                stats[:, base_mv + 2 * ti:base_mv + 2 * ti + 2], stats[:, 0:12])),
                 reads=["statsB"], writes=[reg, "statsB"], tag=tag)
        vv = stats[:, base_mv + 1:base_mv + 4:2]
        uu = stats[:, CUc:CUc + 2]
        if which == 1:
            p.op("vector", (lambda vv=vv, uu=uu: nc.vector.tensor_scalar(
                uu, vv, (D + EPS) / D, (EPS * EPS) / D, OP.mult, OP.add)),
                 reads=[reg], writes=["statsN"], tag=tag)
        else:
            p.op("vector", (lambda vv=vv, uu=uu: nc.vector.tensor_scalar(
                uu, vv, 1.0, EPS, OP.mult, OP.add)),
                 reads=[reg], writes=["statsN"], tag=tag)
        p.op("scalar", (lambda uu=uu: nc.scalar.activation(
            stats[:, CTc:CTc + 2], uu, AF.Sqrt)),
             reads=["statsN"], writes=["statsN"], tag=tag + "_sqrt")
        p.op("vector", (lambda ssc=ssc: nc.vector.reciprocal(
            stats[:, ssc:ssc + 2], stats[:, CTc:CTc + 2])),
             reads=["statsN"], writes=[reg, "statsN"], tag=tag)
        bbc = BB1 if which == 1 else BB2
        mm_ap = stats[:, base_mv:base_mv + 4:2]
        p.op("vector", (lambda ssc=ssc, bbc=bbc, mm_ap=mm_ap:
                        nc.vector.scalar_tensor_tensor(
                            stats[:, bbc:bbc + 2], mm_ap, -1.0,
                            stats[:, ssc:ssc + 2], OP.mult, OP.mult)),
             reads=[reg], writes=[reg], tag=tag)

    def ln_apply(b, which, tag):
        par = b % 3
        ssc = SS1 if which == 1 else SS2
        xreg = "x%d" % par
        reg = "stats%d" % which
        dstreg = ("xhf%d" % (b % 2)) if which == 1 else "xh2"
        bbc = BB1 if which == 1 else BB2
        stg = stg1 if which == 1 else bfP
        streg = "stg1" if which == 1 else "bfP"
        for ti in range(NTI):
            b_ap = stats[:, bbc + ti:bbc + ti + 1]
            m_ap = stats[:, (MV1 if which == 1 else MV2) + 2 * ti:
                         (MV1 if which == 1 else MV2) + 2 * ti + 1]
            s_ap = stats[:, ssc + ti:ssc + ti + 1]
            xt = x_tm[:, par, ti * D:(ti + 1) * D]
            if ti == 0:
                p.op("vector", (lambda xt=xt, m_ap=m_ap, s_ap=s_ap, stg=stg:
                                nc.vector.tensor_scalar(stg[:, 0:D], xt, m_ap, s_ap,
                                                        OP.subtract, OP.mult)),
                     reads=[xreg + "t0", reg], writes=[streg], tag=tag)
            else:
                p.op("scalar", (lambda xt=xt, b_ap=b_ap, s_ap=s_ap, stg=stg:
                                nc.scalar.activation(stg[:, D:2 * D], xt,
                                                     AF.Identity, bias=b_ap, scale=s_ap)),
                     reads=[xreg + "t1", reg], writes=[streg], tag=tag)
        for ti in range(NTI):
            if which == 1:
                dst = xhat_fm[:, b % 2, :].rearrange(
                    "p (c l) -> p c l", c=NCH)[:, :, ti * 128:(ti + 1) * 128]
            else:
                dst = chunked(xh2_fm)[:, :, ti * 128:(ti + 1) * 128]
            p.op("sync", (lambda dst=dst, ti=ti, stg=stg:
                          nc.sync.dma_start_transpose(
                              dst, stg[:, ti * D:(ti + 1) * D])),
                 reads=[streg], writes=[dstreg], dma=True, tag=tag + "_dmaT")

    def load_x(b, eng):
        par = b % 3
        for ti in range(NTI):
            srcap = x_d[b * L + ti * 128: b * L + (ti + 1) * 128, :]
            dst = x_tm[:, par, ti * D:(ti + 1) * D]
            p.op(eng, (lambda dst=dst, srcap=srcap, eng=eng:
                       getattr(nc, eng).dma_start(dst, srcap)),
                 reads=[], writes=["x%dt%d" % (par, ti)], dma=True,
                 tag="load_x%d" % b)

    def small_load(dst, src, reg):
        p.op("sync", (lambda dst=dst, src=src: nc.sync.dma_start(dst, src)),
             reads=[], writes=[reg], dma=True, tag="prep_small")

    load_x(0, "sync")
    load_x(1, "sync")
    load_x(2, "sync")
    small_load(gam[:, :], gam_d[:, :], "gam")
    small_load(mur[:, :], mur_d[:, :], "mur")
    small_load(muk[:, :], muk_d[:, :], "muk")
    small_load(ident[:, :], id_d[:, :], "ident")

    for n in ("wk", "wv", "wa", "wq", "wg", "wo", "wrT", "wkT", "wvT"):
        p.op("gpsimd", (lambda n=n:
                        nc.gpsimd.dma_start(wsb[n][:, :, :], w_d[n][:, :, :])),
             reads=[], writes=["W_" + n], dma=True, tag="cast_" + n)
        if n in ("wk", "wv", "wa", "wq", "wg"):
            for c in range(NCH):
                p.op("vector", (lambda n=n, c=c: nc.vector.tensor_scalar_mul(
                    wsb[n][:, c, :], wsb[n][:, c, :], gam[:, c:c + 1])),
                     reads=["gam", "W_" + n], writes=["W_" + n], tag="gfold_" + n)

    p.op("vector", (lambda: nc.vector.tensor_scalar(
        onemr[:, :], mur[:, :], -1.0, 1.0, OP.mult, OP.add)),
         reads=["mur"], writes=["onemr"], tag="onemr")

    p.op("vector", (lambda: nc.vector.tensor_scalar(
        onemk[:, :], muk[:, :], -1.0, 1.0, OP.mult, OP.add)),
         reads=["muk"], writes=["onemk"], tag="onemk")

    store_toks = []

    def phase_A(b):
        ln_stats(b, 1, "A%d" % b)
        ln_apply(b, 1, "A%d" % b)

    def phase_hoist_a(b):
        rr = ["xhf%d" % (b % 2)]
        rhs = lambda kc, b=b: xhat_view(b, kc)
        a_ids = mm_stage("wa", rhs, rr, "a%d" % b)
        for j in range(4):
            p.op("scalar", (lambda j=j, a_ids=a_ids: nc.scalar.activation(
                fhalf(fA, j), pp[a_ids[j]][:, :], AF.Sigmoid)),
                 reads=["pp%d" % a_ids[j]], writes=["fA%d" % j], tag="siga%d" % b)

    def phase_hoist_kvq_mm(b):
        rr = ["xhf%d" % (b % 2)]
        rhs = lambda kc, b=b: xhat_view(b, kc)
        k_ids = mm_stage("wk", rhs, rr, "k%d" % b)
        for j in range(4):
            p.op("scalar", (lambda j=j, k_ids=k_ids: nc.scalar.copy(
                fhalf(fB, j), pp[k_ids[j]][:, :])),
                 reads=["pp%d" % k_ids[j]], writes=["fB%d" % j], tag="kev%d" % b)
        v_ids = mm_stage("wv", rhs, rr, "v%d" % b)
        q_ids = mm_stage("wq", rhs, rr, "q%d" % b)
        return v_ids, q_ids

    def phase_kvev(b, v_ids):
        for j in range(4):
            p.op("vector", (lambda j=j, v_ids=v_ids: nc.vector.tensor_tensor(
                fhalf(fB, j), pp[v_ids[j]][:, :], fhalf(fB, j), OP.mult)),
                 reads=["pp%d" % v_ids[j], "fB%d" % j], writes=["fB%d" % j],
                 tag="kv%d" % b)

    def phase_scan(b):
        for c in range(NCH):
            init = carry[:, c:c + 1] if b > 0 else 0.0
            rd = ["fA%d" % (c // 2), "fB%d" % (c // 2)] + (["carry"] if b > 0 else [])
            p.op("vector", (lambda c=c, init=init: nc.vector.tensor_tensor_scan(
                f3(fC, c), f3(fA, c), f3(fB, c), init, OP.mult, OP.add)),
                 reads=rd, writes=["fC"], tag="scan%d" % b)
        p.op("vector", (lambda: nc.vector.tensor_copy(
            carry[:, :].rearrange("p (c o) -> p c o", o=1),
            chunked(fC)[:, :, L - 1:L])),
             reads=["fC"], writes=["carry"], tag="carryupd%d" % b)

    def phase_yq(b, q_ids):
        for j in range(4):
            p.op("vector", (lambda j=j, q_ids=q_ids: nc.vector.tensor_tensor(
                fhalf(fC, j), fhalf(fC, j), pp[q_ids[j]][:, :], OP.mult)),
                 reads=["pp%d" % q_ids[j], "fC"], writes=["fC"], tag="yq%d" % b)

    def phase_hoist_g(b):
        rr = ["xhf%d" % (b % 2)]
        rhs = lambda kc, b=b: xhat_view(b, kc)
        g_ids = mm_stage("wg", rhs, rr, "g%d" % b)
        for j in range(4):
            p.op("scalar", (lambda j=j, g_ids=g_ids: nc.scalar.activation(
                fhalf(bfP, j), pp[g_ids[j]][:, :], AF.Sigmoid)),
                 reads=["pp%d" % g_ids[j]], writes=["bfP"], tag="sigg%d" % b)
        return g_ids

    def phase_ygy2(b, g_ids):
        for j in range(4):
            p.op("vector", (lambda j=j, g_ids=g_ids: nc.vector.tensor_tensor(
                fhalf(fC, j), fhalf(fC, j), pp[g_ids[j]][:, :], OP.mult)),
                 reads=["pp%d" % g_ids[j], "fC"], writes=["fC"], tag="yg%d" % b)
        p.op("vector", (lambda: nc.vector.tensor_tensor(
            y2b[:, :], fC[:, :], bfP[:, :], OP.mult)),
             reads=["fC", "bfP"], writes=["y2b"], tag="y2_%d" % b)

    def phase_D(b):
        wo_ids = mm_stage("wo", (lambda kc: chunked(y2b)[:, kc, :]), ["y2b"],
                          "wo%d" % b)
        for j in range(4):
            p.op("scalar", (lambda j=j, wo_ids=wo_ids: nc.scalar.copy(
                fhalf(fB, j), pp[wo_ids[j]][:, :])),
                 reads=["pp%d" % wo_ids[j]], writes=["fB%d" % j], tag="woev%d" % b)
        t_ids = transpose_stage(fB, "fB", "Tattn%d" % b, sub=True)
        resid_stage(b, t_ids, "resid1_%d" % b)

    def phase_E1(b):
        ln_stats(b, 2, "E1_%d" % b)
        ln_apply(b, 2, "E1_%d" % b)

    def phase_E2(b):
        v = chunked(xh2_fm)
        pv = chunked(bfP)
        xr = chunked(y2b)
        p.op("vector", (lambda v=v, pv=pv: nc.vector.tensor_copy(
            pv[:, :, 1:L], v[:, :, 0:L - 1])),
             reads=["xh2"], writes=["bfP"], tag="prev%d" % b)
        if b == 0:
            p.op("vector", (lambda pv=pv: nc.vector.memset(pv[:, :, 0:1], 0.0)),
                 reads=[], writes=["bfP"], tag="prev0_%d" % b)
        else:
            p.op("vector", (lambda pv=pv: nc.vector.tensor_copy(
                pv[:, :, 0:1], carry2[:, :].rearrange("p (c o) -> p c o", o=1))),
                 reads=["carry2"], writes=["bfP"], tag="prev0_%d" % b)
        p.op("vector", (lambda v=v: nc.vector.tensor_copy(
            carry2[:, :].rearrange("p (c o) -> p c o", o=1), v[:, :, L - 1:L])),
             reads=["xh2"], writes=["carry2"], tag="carry2_%d" % b)
        tsh = chunked(stg1)
        p.op("vector", (lambda: nc.vector.tensor_sub(
            stg1[:, :], xh2_fm[:, :], bfP[:, :])),
             reads=["xh2", "bfP"], writes=["stg1"], tag="xr%d" % b)
        for c in range(NCH):
            p.op("vector", (lambda c=c, tsh=tsh, pv=pv, xr=xr:
                            nc.vector.scalar_tensor_tensor(
                                xr[:, c, :], tsh[:, c, :], mur[:, c:c + 1],
                                pv[:, c, :], OP.mult, OP.add)),
                 reads=["stg1", "mur", "bfP"], writes=["y2b"], tag="xr%d" % b)
        for c in range(NCH):
            p.op("vector", (lambda c=c, tsh=tsh, pv=pv, v=v:
                            nc.vector.scalar_tensor_tensor(
                                v[:, c, :], tsh[:, c, :], muk[:, c:c + 1],
                                pv[:, c, :], OP.mult, OP.add)),
                 reads=["stg1", "muk", "bfP"], writes=["xh2"], tag="xk%d" % b)

    def phase_F1(b):
        xr_fn = lambda kc: chunked(y2b)[:, kc, :]
        r_ids = mm_stage("wrT", xr_fn, ["y2b"], "fr%d" % b)
        for j in range(4):
            p.op("scalar", (lambda j=j, r_ids=r_ids: nc.scalar.activation(
                fhalf(fB, j), pp[r_ids[j]][:, :], AF.Sigmoid)),
                 reads=["pp%d" % r_ids[j]], writes=["fB%d" % j], tag="sigr%d" % b)

    def phase_F2(b):
        xk_fn = lambda kc: chunked(xh2_fm)[:, kc, :]
        k_ids = mm_stage("wkT", xk_fn, ["xh2"], "fk%d" % b)
        for j in range(4):
            p.op("scalar", (lambda j=j, k_ids=k_ids: nc.scalar.activation(
                fhalf(y2b, j), pp[k_ids[j]][:, :], AF.Relu)),
                 reads=["pp%d" % k_ids[j]], writes=["y2b"], tag="krelu%d" % b)
        p.op("scalar", (lambda: nc.scalar.activation(
            xh2_fm[:, :], y2b[:, :], AF.Square)),
             reads=["y2b"], writes=["xh2"], tag="ksq%d" % b)

    def phase_F3(b):
        ks_fn = lambda kc: chunked(xh2_fm)[:, kc, :]
        v_ids = mm_stage("wvT", ks_fn, ["xh2"], "fv%d" % b)
        for j in range(4):
            p.op("vector", (lambda j=j, v_ids=v_ids: nc.vector.tensor_tensor(
                fhalf(fB, j), pp[v_ids[j]][:, :], fhalf(fB, j), OP.mult)),
                 reads=["pp%d" % v_ids[j], "fB%d" % j], writes=["fB%d" % j],
                 tag="fvev%d" % b)

    def phase_F4(b):
        t_ids = transpose_stage(fB, "fB", "Tffn%d" % b, sub=True)
        resid_stage(b, t_ids, "resid2_%d" % b)

    def phase_G(b):
        par = b % 3
        dst = out_d[b * L:(b + 1) * L, :].rearrange("(t p) d -> p t d", p=128)
        srcv = x_tm[:, par, :].rearrange("p (t d) -> p t d", t=NTI)
        store_toks.append(
            p.op("sync", (lambda dst=dst, srcv=srcv: nc.sync.dma_start(dst, srcv)),
                 reads=["x%dt0" % par, "x%dt1" % par], writes=[], dma=True,
                 tag="store%d" % b))
        if b + 3 < NB:
            load_x(b + 3, "sync")

    phase_A(0)
    phase_A(1)
    vq = {}
    g_ids_of = {}
    vq[0] = phase_hoist_kvq_mm(0)
    phase_kvev(0, vq[0][0])
    phase_hoist_a(0)
    phase_scan(0)
    phase_yq(0, vq[0][1])
    g_ids_of[0] = phase_hoist_g(0)
    phase_ygy2(0, g_ids_of[0])
    phase_hoist_a(1)
    for b in range(NB):
        phase_D(b)
        phase_E1(b)
        if b + 1 < NB:
            vq[b + 1] = phase_hoist_kvq_mm(b + 1)
        phase_E2(b)
        if b + 2 < NB:
            phase_A(b + 2)
        if b + 1 < NB:
            phase_kvev(b + 1, vq[b + 1][0])
            phase_scan(b + 1)
            phase_yq(b + 1, vq[b + 1][1])
        phase_F1(b)
        phase_F2(b)
        if b + 1 < NB:
            g_ids_of[b + 1] = phase_hoist_g(b + 1)
            phase_ygy2(b + 1, g_ids_of[b + 1])
        phase_F3(b)
        phase_F4(b)
        if b + 2 < NB:
            phase_hoist_a(b + 2)
        phase_G(b)
        if b + 2 < NB:
            phase_A(b + 2)

    p.finalize()

    sems = {}
    sem_guards = []
    sem_names = ["tensor", "vector", "scalar", "gpsimd", "sync"]
    sem_names += ["dq%d" % (i + 1) for i in range(p.counts.get("dq", 0))]
    for name in sem_names:
        g = nc.semaphore("sem_" + name)
        sems[name] = g.__enter__()
        sem_guards.append(g)

    per_engine = {e: [] for e in Plan.ENGINES}
    for rec in p.ops:
        per_engine[rec["engine"]].append(rec)

    final_waits = [(sem, 16) for (sem, _v) in store_toks]

    with nc.Block() as block:
        def run_engine(eng_name, eng):
            for rec in per_engine[eng_name]:
                for (sem, val) in rec["waits"]:
                    eng.wait_ge(sems[sem], val)
                ins = rec["emit"]()
                if rec["inc"]:
                    step = 16 if rec["sem"].startswith("dq") else 1
                    ins.then_inc(sems[rec["sem"]], step)

        @block.tensor
        def _(eng):
            run_engine("tensor", eng)

        @block.vector
        def _(eng):
            run_engine("vector", eng)

        @block.scalar
        def _(eng):
            run_engine("scalar", eng)

        @block.gpsimd
        def _(eng):
            run_engine("gpsimd", eng)

        @block.sync
        def _(eng):
            run_engine("sync", eng)
            for (sem, val) in final_waits:
                eng.wait_ge(sems[sem], val)

    for g in reversed(sem_guards):
        g.__exit__(None, None, None)
    for g in reversed(ctx):
        g.__exit__(None, None, None)
    return nc


def kernel(L0_attn_x, L0_attn_gamma, L0_attn_wq, L0_attn_wk, L0_attn_wv,
           L0_attn_wa, L0_attn_wg, L0_attn_wo, L0_ffn_W_r, L0_ffn_W_k,
           L0_ffn_W_v, L0_ffn_mu_r, L0_ffn_mu_k):
    x = np.ascontiguousarray(np.asarray(L0_attn_x, dtype=np.float32))
    def lay(w):
        return np.ascontiguousarray(
            np.asarray(w, np.float32).reshape(8, 128, 1024).transpose(1, 0, 2))

    com = {
        "gamma": np.ascontiguousarray(
            np.asarray(L0_attn_gamma, np.float32).reshape(8, 128).T),
        "wq": lay(L0_attn_wq),
        "wk": lay(L0_attn_wk),
        "wv": lay(L0_attn_wv),
        "wa": lay(L0_attn_wa),
        "wg": lay(L0_attn_wg),
        "wo": lay(L0_attn_wo),
        "wrT": lay(np.asarray(L0_ffn_W_r, np.float32).T),
        "wkT": lay(np.asarray(L0_ffn_W_k, np.float32).T),
        "wvT": lay(np.asarray(L0_ffn_W_v, np.float32).T),
        "mu_r": np.ascontiguousarray(
            np.asarray(L0_ffn_mu_r, np.float32).reshape(8, 128).T),
        "mu_k": np.ascontiguousarray(
            np.asarray(L0_ffn_mu_k, np.float32).reshape(8, 128).T),
        "ident": np.eye(128, dtype=np.float32),
    }
    nc = build_program()
    in_maps = [dict(com, x=np.ascontiguousarray(x[c])) for c in range(8)]
    res = run_bass_kernel_spmd(nc, in_maps, core_ids=list(range(8)))
    return np.stack([np.asarray(r["out"], dtype=np.float32)
                     for r in res.results], axis=0)


# version 15
# speedup vs baseline: 1.2023x; 1.0045x over previous
import numpy as np
import concourse.bass as bass
import concourse.mybir as mybir
from concourse.bass_utils import run_bass_kernel_spmd

F32 = mybir.dt.float32
BF16 = mybir.dt.bfloat16
AF = mybir.ActivationFunctionType
OP = mybir.AluOpType

D = 1024
T = 2048
L = 256
NB = T // L
NCH = D // 128
NTI = L // 128
EPS = 1e-5


class Plan:

    ENGINES = ("tensor", "vector", "scalar", "gpsimd", "sync")

    def __init__(self):
        self.ops = []
        self.counts = {}
        self.regions = {}
        self.tok_group = {}

    def op(self, engine, emit, reads=(), writes=(), dma=False, tag="",
           group=None):
        if dma:
            self.counts["dq"] = self.counts.get("dq", 0) + 1
            sem = "dq%d" % self.counts["dq"]
            val = 16
        else:
            sem = engine
            val = self.counts.get(sem, 0) + 1
            self.counts[sem] = val
        token = (sem, val)
        deps = set()
        for r in reads:
            reg = self.regions.get(r)
            if reg and reg["w"]:
                deps.add(reg["w"])
            if reg and r.startswith("pp"):
                deps.update(t for t in reg["r"] if t[0] != sem)
        for w in writes:
            reg = self.regions.get(w)
            if reg:
                if reg["w"]:
                    deps.add(reg["w"])
                deps.update(reg["r"])
        if group is not None:
            deps = {d for d in deps if self.tok_group.get(d) != group}
            self.tok_group[token] = group
        self.ops.append(dict(engine=engine, emit=emit, sem=sem, val=val,
                             deps=sorted(deps), dma=dma, tag=tag))
        for r in reads:
            reg = self.regions.setdefault(r, {"w": None, "r": []})
            reg["r"].append(token)
        for w in writes:
            self.regions[w] = {"w": token, "r": []}
        return token

    def finalize(self):
        needed = {}
        for rec in self.ops:
            for (sem, val) in rec["deps"]:
                needed.setdefault(sem, set()).add(val)
        self.needed = {sem: sorted(v) for sem, v in needed.items()}
        self.incval = {}
        for sem, vals in self.needed.items():
            if sem.startswith("dq"):
                continue
            for i, v in enumerate(vals):
                self.incval[(sem, v)] = i + 1
        known = {}
        for rec in self.ops:
            w = []
            for (sem, val) in sorted(rec["deps"]):
                if sem.startswith("dq"):
                    cnt, wv = 16, 16
                else:
                    cnt = self.incval[(sem, val)]
                    wv = cnt
                k = known.setdefault(rec["engine"], {})
                if k.get(sem, 0) < cnt:
                    k[sem] = cnt
                    w.append((sem, wv))
            rec["waits"] = w
            if rec["sem"].startswith("dq"):
                rec["inc"] = True
            else:
                rec["inc"] = rec["val"] in set(self.needed.get(rec["sem"], []))


def build_program():
    nc = bass.Bass("TRN2", target_bir_lowering=False,
                   dynamic_dma_scratch_size=4096)

    x_d = nc.dram_tensor("x", [T, D], F32, kind="ExternalInput")
    gam_d = nc.dram_tensor("gamma", [128, NCH], F32, kind="ExternalInput")
    w_d = {}
    for n in ("wq", "wk", "wv", "wa", "wg", "wo", "wrT", "wkT", "wvT"):
        w_d[n] = nc.dram_tensor(n, [128, NCH, D], F32, kind="ExternalInput")
    mur_d = nc.dram_tensor("mu_r", [128, NCH], F32, kind="ExternalInput")
    muk_d = nc.dram_tensor("mu_k", [128, NCH], F32, kind="ExternalInput")
    id_d = nc.dram_tensor("ident", [128, 128], F32, kind="ExternalInput")
    out_d = nc.dram_tensor("out", [T, D], F32, kind="ExternalOutput")

    ctx = []

    def sb(name, shape, dtype):
        g = nc.sbuf_tensor(name, shape, dtype)
        t = g.__enter__()
        ctx.append(g)
        return t

    def psum(name, shape, dtype):
        g = nc.psum_tensor(name, shape, dtype)
        t = g.__enter__()
        ctx.append(g)
        return t

    wsb = {}
    for n in ("wq", "wk", "wv", "wa", "wg", "wo", "wrT", "wkT", "wvT"):
        wsb[n] = sb("W_" + n, [128, NCH, D], BF16)
    x_tm = sb("x_tm", [128, 3, NTI * D], F32)
    xhat_fm = sb("xhat_fm", [128, 2, NCH * L], BF16)
    xh2_fm = sb("xh2_fm", [128, NCH * L], BF16)
    bfP = sb("bfP", [128, NCH * L], BF16)
    stg1 = sb("stg1", [128, NTI * D], BF16)
    y2b = sb("y2b", [128, NCH * L], BF16)
    fA = sb("fA", [128, NCH * L], F32)
    fB = sb("fB", [128, NCH * L], F32)
    fC = sb("fC", [128, NCH * L], F32)
    stats = sb("stats", [128, 36], F32)
    gam = sb("gam", [128, NCH], F32)
    mur = sb("mur", [128, NCH], F32)
    muk = sb("muk", [128, NCH], F32)
    onemr = sb("onemr", [128, NCH], F32)
    onemk = sb("onemk", [128, NCH], F32)
    carry = sb("carry", [128, NCH], F32)
    carry2 = sb("carry2", [128, NCH], BF16)
    ident = sb("identity", [128, 128], F32)

    pp = [psum("pp%d" % i, [128, 512], F32) for i in range(8)]

    p = Plan()

    def wview(n, kc, dlo, dhi):
        return wsb[n][:, kc, dlo:dhi]

    def xhat_view(b, c):
        return xhat_fm[:, b % 2, c * L:(c + 1) * L]

    def f3(t, c):
        return t[:, c * L:(c + 1) * L]

    def fhalf(t, j):
        return t[:, j * 512:(j + 1) * 512]

    def chunked(t):
        return t[:, :].rearrange("p (c l) -> p c l", c=NCH)

    BN = 0
    MV1, MV2 = 12, 16
    SS1, SS2 = 20, 22
    CU, CT, CR, CQ1 = 24, 25, 26, 27

    ppi = [0]

    def next_pps():
        base = ppi[0]
        ids = [(base + i) % 8 for i in range(4)]
        ppi[0] = (base + 4) % 8
        return ids

    def mm_stage(wname, rhs_fn, rhs_regions, tag):
        ids = next_pps()
        for j in range(4):
            pj = pp[ids[j]]
            for h in range(2):
                douc = 2 * j + h
                for kc in range(NCH):
                    p.op("tensor",
                         (lambda pj=pj, h=h, wname=wname, kc=kc, douc=douc,
                                 st=(kc == 0), sp=(kc == NCH - 1), rhs_fn=rhs_fn:
                          nc.tensor.matmul(pj[:, h * 256:(h + 1) * 256],
                                           wview(wname, kc, douc * 128, (douc + 1) * 128),
                                           rhs_fn(kc), start=st, stop=sp)),
                         reads=["W_" + wname] + list(rhs_regions),
                         writes=["pp%d" % ids[j]], tag=tag,
                         group=tag + "_p%d" % ids[j])
        return ids

    def transpose_stage(srcbuf, srcreg, tag, sub=False):
        ids = next_pps()
        for c in range(NCH):
            for ti in range(NTI):
                pj = pp[ids[2 * ti + c // 4]]
                src = chunked(srcbuf)[:, c, ti * 128:(ti + 1) * 128]
                rd = [("%s%d" % (srcreg, c // 2)) if sub else srcreg, "ident"]
                p.op("tensor", (lambda pj=pj, src=src, c=c: nc.tensor.transpose(
                    pj[:, (c % 4) * 128:(c % 4 + 1) * 128], src, ident[:, :])),
                     reads=rd,
                     writes=["pp%d" % ids[2 * ti + c // 4]], tag=tag,
                     group=tag + "_p%d" % ids[2 * ti + c // 4])
        return ids

    def resid_stage(b, t_ids, tag):
        par = b % 3
        for ti in range(NTI):
            for hh in range(2):
                j = 2 * ti + hh
                xv = x_tm[:, par, ti * D + hh * 512: ti * D + (hh + 1) * 512]
                xrg = "x%dt%d" % (par, ti)
                p.op("vector", (lambda xv=xv, j=j, t_ids=t_ids:
                                nc.vector.tensor_tensor(xv, xv, pp[t_ids[j]][:, :], OP.add)),
                     reads=["pp%d" % t_ids[j], xrg], writes=[xrg],
                     tag=tag)

    MV1, MV2, SS1, SS2 = 12, 16, 20, 22
    BB1, BB2 = 32, 34
    CUc, CTc, CRc, CQ1c = 24, 26, 28, 30

    def ln_stats(b, which, tag):
        par = b % 3
        base_mv = MV1 if which == 1 else MV2
        ssc = SS1 if which == 1 else SS2
        xreg = "x%d" % par
        reg = "stats%d" % which
        for ti in range(NTI):
            xt = x_tm[:, par, ti * D:(ti + 1) * D]
            xrg = "%st%d" % (xreg, ti)
            p.op("vector", (lambda xt=xt: nc.vector.bn_stats(
                stats[:, 0:6], xt[:, 0:512])),
                 reads=[xrg], writes=["statsB"], tag=tag)
            p.op("vector", (lambda xt=xt: nc.vector.bn_stats(
                stats[:, 6:12], xt[:, 512:1024])),
                 reads=[xrg], writes=["statsB"], tag=tag)
            p.op("vector", (lambda base_mv=base_mv, ti=ti: nc.vector.bn_aggr(
                stats[:, base_mv + 2 * ti:base_mv + 2 * ti + 2], stats[:, 0:12])),
                 reads=["statsB"], writes=[reg, "statsB"], tag=tag)
        vv = stats[:, base_mv + 1:base_mv + 4:2]
        uu = stats[:, CUc:CUc + 2]
        if which == 1:
            p.op("vector", (lambda vv=vv, uu=uu: nc.vector.tensor_scalar(
                uu, vv, (D + EPS) / D, (EPS * EPS) / D, OP.mult, OP.add)),
                 reads=[reg], writes=["statsN"], tag=tag)
        else:
            p.op("vector", (lambda vv=vv, uu=uu: nc.vector.tensor_scalar(
                uu, vv, 1.0, EPS, OP.mult, OP.add)),
                 reads=[reg], writes=["statsN"], tag=tag)
        p.op("scalar", (lambda uu=uu: nc.scalar.activation(
            stats[:, CTc:CTc + 2], uu, AF.Sqrt)),
             reads=["statsN"], writes=["statsN"], tag=tag + "_sqrt")
        p.op("vector", (lambda ssc=ssc: nc.vector.reciprocal(
            stats[:, ssc:ssc + 2], stats[:, CTc:CTc + 2])),
             reads=["statsN"], writes=[reg, "statsN"], tag=tag)
        bbc = BB1 if which == 1 else BB2
        mm_ap = stats[:, base_mv:base_mv + 4:2]
        p.op("vector", (lambda ssc=ssc, bbc=bbc, mm_ap=mm_ap:
                        nc.vector.scalar_tensor_tensor(
                            stats[:, bbc:bbc + 2], mm_ap, -1.0,
                            stats[:, ssc:ssc + 2], OP.mult, OP.mult)),
             reads=[reg], writes=[reg], tag=tag)

    def ln_apply(b, which, tag):
        par = b % 3
        ssc = SS1 if which == 1 else SS2
        xreg = "x%d" % par
        reg = "stats%d" % which
        dstreg = ("xhf%d" % (b % 2)) if which == 1 else "xh2"
        bbc = BB1 if which == 1 else BB2
        stg = stg1 if which == 1 else bfP
        streg = "stg1" if which == 1 else "bfP"
        for ti in range(NTI):
            b_ap = stats[:, bbc + ti:bbc + ti + 1]
            m_ap = stats[:, (MV1 if which == 1 else MV2) + 2 * ti:
                         (MV1 if which == 1 else MV2) + 2 * ti + 1]
            s_ap = stats[:, ssc + ti:ssc + ti + 1]
            xt = x_tm[:, par, ti * D:(ti + 1) * D]
            if ti == 0:
                p.op("vector", (lambda xt=xt, m_ap=m_ap, s_ap=s_ap, stg=stg:
                                nc.vector.tensor_scalar(stg[:, 0:D], xt, m_ap, s_ap,
                                                        OP.subtract, OP.mult)),
                     reads=[xreg + "t0", reg], writes=[streg], tag=tag)
            else:
                p.op("scalar", (lambda xt=xt, b_ap=b_ap, s_ap=s_ap, stg=stg:
                                nc.scalar.activation(stg[:, D:2 * D], xt,
                                                     AF.Identity, bias=b_ap, scale=s_ap)),
                     reads=[xreg + "t1", reg], writes=[streg], tag=tag)
        for ti in range(NTI):
            if which == 1:
                dst = xhat_fm[:, b % 2, :].rearrange(
                    "p (c l) -> p c l", c=NCH)[:, :, ti * 128:(ti + 1) * 128]
            else:
                dst = chunked(xh2_fm)[:, :, ti * 128:(ti + 1) * 128]
            p.op("sync", (lambda dst=dst, ti=ti, stg=stg:
                          nc.sync.dma_start_transpose(
                              dst, stg[:, ti * D:(ti + 1) * D])),
                 reads=[streg], writes=[dstreg], dma=True, tag=tag + "_dmaT")

    def load_x(b, eng):
        par = b % 3
        for ti in range(NTI):
            srcap = x_d[b * L + ti * 128: b * L + (ti + 1) * 128, :]
            dst = x_tm[:, par, ti * D:(ti + 1) * D]
            p.op(eng, (lambda dst=dst, srcap=srcap, eng=eng:
                       getattr(nc, eng).dma_start(dst, srcap)),
                 reads=[], writes=["x%dt%d" % (par, ti)], dma=True,
                 tag="load_x%d" % b)

    def small_load(dst, src, reg):
        p.op("sync", (lambda dst=dst, src=src: nc.sync.dma_start(dst, src)),
             reads=[], writes=[reg], dma=True, tag="prep_small")

    load_x(0, "sync")
    load_x(1, "sync")
    load_x(2, "sync")
    small_load(gam[:, :], gam_d[:, :], "gam")
    small_load(mur[:, :], mur_d[:, :], "mur")
    small_load(muk[:, :], muk_d[:, :], "muk")
    small_load(ident[:, :], id_d[:, :], "ident")

    for n in ("wk", "wv", "wa", "wq", "wg", "wo", "wrT", "wkT", "wvT"):
        p.op("gpsimd", (lambda n=n:
                        nc.gpsimd.dma_start(wsb[n][:, :, :], w_d[n][:, :, :])),
             reads=[], writes=["W_" + n], dma=True, tag="cast_" + n)
        if n in ("wk", "wv", "wa", "wq", "wg"):
            for c in range(NCH):
                p.op("vector", (lambda n=n, c=c: nc.vector.tensor_scalar_mul(
                    wsb[n][:, c, :], wsb[n][:, c, :], gam[:, c:c + 1])),
                     reads=["gam", "W_" + n], writes=["W_" + n], tag="gfold_" + n)

    p.op("vector", (lambda: nc.vector.tensor_scalar(
        onemr[:, :], mur[:, :], -1.0, 1.0, OP.mult, OP.add)),
         reads=["mur"], writes=["onemr"], tag="onemr")

    p.op("vector", (lambda: nc.vector.tensor_scalar(
        onemk[:, :], muk[:, :], -1.0, 1.0, OP.mult, OP.add)),
         reads=["muk"], writes=["onemk"], tag="onemk")

    store_toks = []

    def phase_A(b):
        ln_stats(b, 1, "A%d" % b)
        ln_apply(b, 1, "A%d" % b)

    def phase_hoist_a(b):
        rr = ["xhf%d" % (b % 2)]
        rhs = lambda kc, b=b: xhat_view(b, kc)
        a_ids = mm_stage("wa", rhs, rr, "a%d" % b)
        for j in range(4):
            p.op("scalar", (lambda j=j, a_ids=a_ids: nc.scalar.activation(
                fhalf(fA, j), pp[a_ids[j]][:, :], AF.Sigmoid)),
                 reads=["pp%d" % a_ids[j]], writes=["fA%d" % j], tag="siga%d" % b)

    def phase_hoist_kvq_mm(b):
        rr = ["xhf%d" % (b % 2)]
        rhs = lambda kc, b=b: xhat_view(b, kc)
        k_ids = mm_stage("wk", rhs, rr, "k%d" % b)
        for j in range(4):
            p.op("scalar", (lambda j=j, k_ids=k_ids: nc.scalar.copy(
                fhalf(fB, j), pp[k_ids[j]][:, :])),
                 reads=["pp%d" % k_ids[j]], writes=["fB%d" % j], tag="kev%d" % b)
        v_ids = mm_stage("wv", rhs, rr, "v%d" % b)
        q_ids = mm_stage("wq", rhs, rr, "q%d" % b)
        return v_ids, q_ids

    def phase_kvev(b, v_ids):
        for j in range(4):
            p.op("vector", (lambda j=j, v_ids=v_ids: nc.vector.tensor_tensor(
                fhalf(fB, j), pp[v_ids[j]][:, :], fhalf(fB, j), OP.mult)),
                 reads=["pp%d" % v_ids[j], "fB%d" % j], writes=["fB%d" % j],
                 tag="kv%d" % b)

    def phase_scan(b):
        for c in range(NCH):
            init = carry[:, c:c + 1] if b > 0 else 0.0
            rd = ["fA%d" % (c // 2), "fB%d" % (c // 2)] + (["carry"] if b > 0 else [])
            p.op("vector", (lambda c=c, init=init: nc.vector.tensor_tensor_scan(
                f3(fC, c), f3(fA, c), f3(fB, c), init, OP.mult, OP.add)),
                 reads=rd, writes=["fC"], tag="scan%d" % b)
        p.op("vector", (lambda: nc.vector.tensor_copy(
            carry[:, :].rearrange("p (c o) -> p c o", o=1),
            chunked(fC)[:, :, L - 1:L])),
             reads=["fC"], writes=["carry"], tag="carryupd%d" % b)

    def phase_yq(b, q_ids):
        for j in range(4):
            p.op("vector", (lambda j=j, q_ids=q_ids: nc.vector.tensor_tensor(
                fhalf(fC, j), fhalf(fC, j), pp[q_ids[j]][:, :], OP.mult)),
                 reads=["pp%d" % q_ids[j], "fC"], writes=["fC"], tag="yq%d" % b)

    def phase_hoist_g(b):
        rr = ["xhf%d" % (b % 2)]
        rhs = lambda kc, b=b: xhat_view(b, kc)
        g_ids = mm_stage("wg", rhs, rr, "g%d" % b)
        for j in range(4):
            p.op("scalar", (lambda j=j, g_ids=g_ids: nc.scalar.activation(
                fhalf(bfP, j), pp[g_ids[j]][:, :], AF.Sigmoid)),
                 reads=["pp%d" % g_ids[j]], writes=["bfP"], tag="sigg%d" % b)
        return g_ids

    def phase_ygy2(b, g_ids):
        for j in range(4):
            p.op("vector", (lambda j=j, g_ids=g_ids: nc.vector.tensor_tensor(
                fhalf(fC, j), fhalf(fC, j), pp[g_ids[j]][:, :], OP.mult)),
                 reads=["pp%d" % g_ids[j], "fC"], writes=["fC"], tag="yg%d" % b)
        p.op("vector", (lambda: nc.vector.tensor_tensor(
            y2b[:, :], fC[:, :], bfP[:, :], OP.mult)),
             reads=["fC", "bfP"], writes=["y2b"], tag="y2_%d" % b)

    def phase_D(b):
        wo_ids = mm_stage("wo", (lambda kc: chunked(y2b)[:, kc, :]), ["y2b"],
                          "wo%d" % b)
        for j in range(4):
            p.op("scalar", (lambda j=j, wo_ids=wo_ids: nc.scalar.copy(
                fhalf(fB, j), pp[wo_ids[j]][:, :])),
                 reads=["pp%d" % wo_ids[j]], writes=["fB%d" % j], tag="woev%d" % b)
        t_ids = transpose_stage(fB, "fB", "Tattn%d" % b, sub=True)
        resid_stage(b, t_ids, "resid1_%d" % b)

    def phase_E1(b):
        ln_stats(b, 2, "E1_%d" % b)
        ln_apply(b, 2, "E1_%d" % b)

    def phase_E2xr(b):
        v = chunked(xh2_fm)
        pv = chunked(bfP)
        xr = chunked(y2b)
        p.op("vector", (lambda v=v, pv=pv: nc.vector.tensor_copy(
            pv[:, :, 1:L], v[:, :, 0:L - 1])),
             reads=["xh2"], writes=["bfP"], tag="prev%d" % b)
        if b == 0:
            p.op("vector", (lambda pv=pv: nc.vector.memset(pv[:, :, 0:1], 0.0)),
                 reads=[], writes=["bfP"], tag="prev0_%d" % b)
        else:
            p.op("vector", (lambda pv=pv: nc.vector.tensor_copy(
                pv[:, :, 0:1], carry2[:, :].rearrange("p (c o) -> p c o", o=1))),
                 reads=["carry2"], writes=["bfP"], tag="prev0_%d" % b)
        p.op("vector", (lambda v=v: nc.vector.tensor_copy(
            carry2[:, :].rearrange("p (c o) -> p c o", o=1), v[:, :, L - 1:L])),
             reads=["xh2"], writes=["carry2"], tag="carry2_%d" % b)
        tsh = chunked(stg1)
        p.op("vector", (lambda: nc.vector.tensor_sub(
            stg1[:, :], xh2_fm[:, :], bfP[:, :])),
             reads=["xh2", "bfP"], writes=["stg1"], tag="xr%d" % b)
        for c in range(NCH):
            p.op("vector", (lambda c=c, tsh=tsh, pv=pv, xr=xr:
                            nc.vector.scalar_tensor_tensor(
                                xr[:, c, :], tsh[:, c, :], mur[:, c:c + 1],
                                pv[:, c, :], OP.mult, OP.add)),
                 reads=["stg1", "mur", "bfP"], writes=["y2b"], tag="xr%d" % b)
    def phase_E2xk(b):
        v = chunked(xh2_fm)
        pv = chunked(bfP)
        tsh = chunked(stg1)
        for c in range(NCH):
            p.op("vector", (lambda c=c, tsh=tsh, pv=pv, v=v:
                            nc.vector.scalar_tensor_tensor(
                                v[:, c, :], tsh[:, c, :], muk[:, c:c + 1],
                                pv[:, c, :], OP.mult, OP.add)),
                 reads=["stg1", "muk", "bfP"], writes=["xh2"], tag="xk%d" % b)

    def phase_F1(b):
        xr_fn = lambda kc: chunked(y2b)[:, kc, :]
        r_ids = mm_stage("wrT", xr_fn, ["y2b"], "fr%d" % b)
        for j in range(4):
            p.op("scalar", (lambda j=j, r_ids=r_ids: nc.scalar.activation(
                fhalf(fB, j), pp[r_ids[j]][:, :], AF.Sigmoid)),
                 reads=["pp%d" % r_ids[j]], writes=["fB%d" % j], tag="sigr%d" % b)

    def phase_F2(b):
        xk_fn = lambda kc: chunked(xh2_fm)[:, kc, :]
        k_ids = mm_stage("wkT", xk_fn, ["xh2"], "fk%d" % b)
        for j in range(4):
            p.op("scalar", (lambda j=j, k_ids=k_ids: nc.scalar.activation(
                fhalf(y2b, j), pp[k_ids[j]][:, :], AF.Relu)),
                 reads=["pp%d" % k_ids[j]], writes=["y2b"], tag="krelu%d" % b)
        p.op("scalar", (lambda: nc.scalar.activation(
            xh2_fm[:, :], y2b[:, :], AF.Square)),
             reads=["y2b"], writes=["xh2"], tag="ksq%d" % b)

    def phase_F3(b):
        ks_fn = lambda kc: chunked(xh2_fm)[:, kc, :]
        v_ids = mm_stage("wvT", ks_fn, ["xh2"], "fv%d" % b)
        for j in range(4):
            p.op("vector", (lambda j=j, v_ids=v_ids: nc.vector.tensor_tensor(
                fhalf(fB, j), pp[v_ids[j]][:, :], fhalf(fB, j), OP.mult)),
                 reads=["pp%d" % v_ids[j], "fB%d" % j], writes=["fB%d" % j],
                 tag="fvev%d" % b)

    def phase_F4(b):
        t_ids = transpose_stage(fB, "fB", "Tffn%d" % b, sub=True)
        resid_stage(b, t_ids, "resid2_%d" % b)

    def phase_G(b):
        par = b % 3
        dst = out_d[b * L:(b + 1) * L, :].rearrange("(t p) d -> p t d", p=128)
        srcv = x_tm[:, par, :].rearrange("p (t d) -> p t d", t=NTI)
        store_toks.append(
            p.op("sync", (lambda dst=dst, srcv=srcv: nc.sync.dma_start(dst, srcv)),
                 reads=["x%dt0" % par, "x%dt1" % par], writes=[], dma=True,
                 tag="store%d" % b))
        if b + 3 < NB:
            load_x(b + 3, "sync")

    phase_A(0)
    phase_A(1)
    vq = {}
    g_ids_of = {}
    vq[0] = phase_hoist_kvq_mm(0)
    phase_kvev(0, vq[0][0])
    phase_hoist_a(0)
    phase_scan(0)
    phase_yq(0, vq[0][1])
    g_ids_of[0] = phase_hoist_g(0)
    phase_ygy2(0, g_ids_of[0])
    phase_hoist_a(1)
    for b in range(NB):
        phase_D(b)
        phase_E1(b)
        if b + 1 < NB:
            vq[b + 1] = phase_hoist_kvq_mm(b + 1)
        phase_E2xr(b)
        if b + 1 < NB:
            phase_kvev(b + 1, vq[b + 1][0])
        phase_E2xk(b)
        if b + 1 < NB:
            phase_scan(b + 1)
            phase_yq(b + 1, vq[b + 1][1])
        if b + 2 < NB:
            phase_A(b + 2)
        phase_F1(b)
        phase_F2(b)
        if b + 1 < NB:
            g_ids_of[b + 1] = phase_hoist_g(b + 1)
            phase_ygy2(b + 1, g_ids_of[b + 1])
        phase_F3(b)
        phase_F4(b)
        if b + 2 < NB:
            phase_hoist_a(b + 2)
        phase_G(b)
        if b + 2 < NB:
            phase_A(b + 2)

    p.finalize()

    sems = {}
    sem_guards = []
    sem_names = ["tensor", "vector", "scalar", "gpsimd", "sync"]
    sem_names += ["dq%d" % (i + 1) for i in range(p.counts.get("dq", 0))]
    for name in sem_names:
        g = nc.semaphore("sem_" + name)
        sems[name] = g.__enter__()
        sem_guards.append(g)

    per_engine = {e: [] for e in Plan.ENGINES}
    for rec in p.ops:
        per_engine[rec["engine"]].append(rec)

    final_waits = [(sem, 16) for (sem, _v) in store_toks]

    with nc.Block() as block:
        def run_engine(eng_name, eng):
            for rec in per_engine[eng_name]:
                for (sem, val) in rec["waits"]:
                    eng.wait_ge(sems[sem], val)
                ins = rec["emit"]()
                if rec["inc"]:
                    step = 16 if rec["sem"].startswith("dq") else 1
                    ins.then_inc(sems[rec["sem"]], step)

        @block.tensor
        def _(eng):
            run_engine("tensor", eng)

        @block.vector
        def _(eng):
            run_engine("vector", eng)

        @block.scalar
        def _(eng):
            run_engine("scalar", eng)

        @block.gpsimd
        def _(eng):
            run_engine("gpsimd", eng)

        @block.sync
        def _(eng):
            run_engine("sync", eng)
            for (sem, val) in final_waits:
                eng.wait_ge(sems[sem], val)

    for g in reversed(sem_guards):
        g.__exit__(None, None, None)
    for g in reversed(ctx):
        g.__exit__(None, None, None)
    return nc


def kernel(L0_attn_x, L0_attn_gamma, L0_attn_wq, L0_attn_wk, L0_attn_wv,
           L0_attn_wa, L0_attn_wg, L0_attn_wo, L0_ffn_W_r, L0_ffn_W_k,
           L0_ffn_W_v, L0_ffn_mu_r, L0_ffn_mu_k):
    x = np.ascontiguousarray(np.asarray(L0_attn_x, dtype=np.float32))
    def lay(w):
        return np.ascontiguousarray(
            np.asarray(w, np.float32).reshape(8, 128, 1024).transpose(1, 0, 2))

    com = {
        "gamma": np.ascontiguousarray(
            np.asarray(L0_attn_gamma, np.float32).reshape(8, 128).T),
        "wq": lay(L0_attn_wq),
        "wk": lay(L0_attn_wk),
        "wv": lay(L0_attn_wv),
        "wa": lay(L0_attn_wa),
        "wg": lay(L0_attn_wg),
        "wo": lay(L0_attn_wo),
        "wrT": lay(np.asarray(L0_ffn_W_r, np.float32).T),
        "wkT": lay(np.asarray(L0_ffn_W_k, np.float32).T),
        "wvT": lay(np.asarray(L0_ffn_W_v, np.float32).T),
        "mu_r": np.ascontiguousarray(
            np.asarray(L0_ffn_mu_r, np.float32).reshape(8, 128).T),
        "mu_k": np.ascontiguousarray(
            np.asarray(L0_ffn_mu_k, np.float32).reshape(8, 128).T),
        "ident": np.eye(128, dtype=np.float32),
    }
    nc = build_program()
    in_maps = [dict(com, x=np.ascontiguousarray(x[c])) for c in range(8)]
    res = run_bass_kernel_spmd(nc, in_maps, core_ids=list(range(8)))
    return np.stack([np.asarray(r["out"], dtype=np.float32)
                     for r in res.results], axis=0)
